# version 34
# baseline (speedup 1.0000x reference)
import numpy as np
import ml_dtypes
import concourse.bass as bass
import concourse.mybir as mybir
from concourse.bass_utils import run_bass_kernel_spmd
from contextlib import ExitStack

F32 = mybir.dt.float32
BF16 = mybir.dt.bfloat16
AF = mybir.ActivationFunctionType
ALU = mybir.AluOpType
AX = mybir.AxisListType

D, DC, P, NT, SEQ = 1024, 8, 2048, 16, 2
H, JC = 4096, 32
NG = 4
NFG = 16

K0, V0, Q0 = 0, 16384, 32768
WPJT0, R2T0, H20, GB0 = 0, 32768, 40960, 45056
RLB0, TMP0 = 47104, 47616
WQT0, WKT0, WVT0, WST0, GA0, BV0 = 0, 8192, 16384, 24576, 28672, 30720


class Ctr:
    __slots__ = ("sem", "n")

    def __init__(self, sem):
        self.sem = sem
        self.n = 0


def build():
    nc = bass.Bass("TRN2", target_bir_lowering=False)

    xin = nc.dram_tensor("xin", [SEQ, P, D], F32, kind="ExternalInput")
    wq = nc.dram_tensor("wq", [D, D], F32, kind="ExternalInput")
    wk = nc.dram_tensor("wk", [D, D], F32, kind="ExternalInput")
    wv = nc.dram_tensor("wv", [D, D], F32, kind="ExternalInput")
    bq = nc.dram_tensor("bq", [D], F32, kind="ExternalInput")
    bk = nc.dram_tensor("bk", [D], F32, kind="ExternalInput")
    bv = nc.dram_tensor("bv", [D], F32, kind="ExternalInput")
    wmod = nc.dram_tensor("wmod", [D, 6 * D], F32, kind="ExternalInput")
    cvec = nc.dram_tensor("cvec", [SEQ, D], F32, kind="ExternalInput")
    wfc = nc.dram_tensor("wfc", [H, D], F32, kind="ExternalInput")
    wpj = nc.dram_tensor("wpj", [D, H], F32, kind="ExternalInput")
    uu = nc.dram_tensor("uu", [H, 16], F32, kind="ExternalInput")
    vlr = nc.dram_tensor("vlr", [16, D], F32, kind="ExternalInput")
    fb = nc.dram_tensor("fb", [16, 16], F32, kind="ExternalInput")
    gate16 = nc.dram_tensor("gate16", [16, 1], F32, kind="ExternalInput")
    id_bf = nc.dram_tensor("id_bf", [128, 128], BF16, kind="ExternalInput")
    id_f32 = nc.dram_tensor("id_f32", [16, 16], F32, kind="ExternalInput")
    mask_bf = nc.dram_tensor("mask_bf", [128, 128], BF16, kind="ExternalInput")
    yout = nc.dram_tensor("yout", [SEQ, P, D], F32, kind="ExternalOutput")
    x1s = nc.dram_tensor("x1s", [SEQ, P, D], BF16)

    ops = []

    with ExitStack() as ctx:
        en = ctx.enter_context

        KVQ = en(nc.sbuf_tensor("KVQ", [128, 49152], BF16))
        WPJ = en(nc.sbuf_tensor("WPJ", [128, 32768], BF16))
        C = en(nc.sbuf_tensor("C", [128, 4096], BF16))
        STG = en(nc.sbuf_tensor("STG", [128, 2048], F32))
        xs = [en(nc.sbuf_tensor(f"xs{i}", [128, 1024], BF16)) for i in range(4)]
        outst = [en(nc.sbuf_tensor(f"outst{i}", [128, 512], F32)) for i in range(2)]
        x1r = [en(nc.sbuf_tensor(f"x1r{i}", [128, 512], BF16)) for i in range(2)]
        xh = [en(nc.sbuf_tensor(f"xh{i}", [128, 1024], BF16)) for i in range(2)]
        we1 = en(nc.sbuf_tensor("we1", [128, 1024], BF16))
        Pb = en(nc.sbuf_tensor("Pb", [128, 2048], BF16))
        Pb2 = en(nc.sbuf_tensor("Pb2", [128, 1792], BF16))
        PT = en(nc.sbuf_tensor("PT", [128, 512], BF16))
        msk = en(nc.sbuf_tensor("msk", [128, 128], BF16))
        idb = en(nc.sbuf_tensor("idb", [128, 128], BF16))
        idf = en(nc.sbuf_tensor("idf", [16, 16], F32))
        onesb = en(nc.sbuf_tensor("onesb", [1, 128], BF16))
        bq32 = en(nc.sbuf_tensor("bq32", [128, 8], F32))
        bkT = en(nc.sbuf_tensor("bkT", [128, 8], F32))
        modT = en(nc.sbuf_tensor("modT", [128, 96], F32))
        caP1 = en(nc.sbuf_tensor("caP1", [128, 16], F32))
        saT = en(nc.sbuf_tensor("saT", [128, 16], F32))
        cbP1 = en(nc.sbuf_tensor("cbP1", [128, 16], F32))
        sbT = en(nc.sbuf_tensor("sbT", [128, 16], F32))
        gaTc = en(nc.sbuf_tensor("gaTc", [128, 16], F32))
        gbTc = en(nc.sbuf_tensor("gbTc", [128, 16], F32))
        scT = en(nc.sbuf_tensor("scT", [128, 16], BF16))
        scT32 = en(nc.sbuf_tensor("scT32", [128, 16], F32))
        vlrT = en(nc.sbuf_tensor("vlrT", [128, 128], BF16))
        fbT = en(nc.sbuf_tensor("fbT", [16, 16], BF16))
        fb_bf = en(nc.sbuf_tensor("fb_bf", [16, 16], BF16))
        gate_sb = en(nc.sbuf_tensor("gate_sb", [16, 1], F32))
        stt = [en(nc.sbuf_tensor(f"stt{i}", [128, 8], F32)) for i in range(2)]
        lsum = en(nc.sbuf_tensor("lsum", [128, 8], F32))
        lred = [en(nc.sbuf_tensor(f"lred{i}", [128, 1], F32)) for i in range(2)]
        linv = [en(nc.sbuf_tensor(f"linv{i}", [128, 1], F32)) for i in range(2)]
        tsc = [en(nc.sbuf_tensor(f"tsc{i}", [128, 4], F32)) for i in range(4)]
        scpj = en(nc.sbuf_tensor("scpj", [128, 8], F32))
        pabs = en(nc.sbuf_tensor("pabs", [128, 8], F32))
        kon = en(nc.sbuf_tensor("kon", [128, 2], F32))
        pb = [en(nc.psum_tensor(f"pb{i}", [128, 512], F32)) for i in range(6)]
        pbh = [en(nc.psum_tensor(f"pbh{i}", [128, 1024], BF16)) for i in range(2)]

        cW = Ctr(en(nc.semaphore("sW")))
        cT = Ctr(en(nc.semaphore("sT")))
        cV = Ctr(en(nc.semaphore("sV")))
        cS = Ctr(en(nc.semaphore("sS")))
        cX = [Ctr(en(nc.semaphore(f"sX{i}"))) for i in range(4)]
        cXB = [Ctr(en(nc.semaphore(f"sXB{i}"))) for i in range(4)]
        cSP = [Ctr(en(nc.semaphore(f"sSP{i}"))) for i in range(3)]
        cWS = [Ctr(en(nc.semaphore(f"sWS{i}"))) for i in range(4)]
        cST = [Ctr(en(nc.semaphore(f"sST{i}"))) for i in range(2)]
        cOS = [Ctr(en(nc.semaphore(f"sOS{i}"))) for i in range(2)]
        cWM = Ctr(en(nc.semaphore("sWM")))
        cBV = Ctr(en(nc.semaphore("sBV")))
        cFB = Ctr(en(nc.semaphore("sFB")))
        cVL = Ctr(en(nc.semaphore("sVL")))
        cU = Ctr(en(nc.semaphore("sU")))
        cGC = Ctr(en(nc.semaphore("sGC")))

        def _coal(waits):
            coal = {}
            for (c, v) in waits:
                assert v <= c.n, f"wait ahead of emission: {v} > {c.n}"
                if v > 0:
                    prev = coal.get(id(c))
                    coal[id(c)] = (c, v if prev is None else max(v, prev[1]))
            return tuple(coal.values())

        class _Rec:
            __slots__ = ("e", "last")

            def __init__(self, e):
                self.e = e
                self.last = None

            def __getattr__(self, name):
                m = getattr(self.e, name)

                def call(*a, **k):
                    r = m(*a, **k)
                    self.last = r
                    return r

                return call

        def job(ek, waits, body, inc):
            if ek == "v":
                waits = list(waits) + [(cV, cV.n)]
            elif ek in ("s", "sc"):
                waits = list(waits) + [(cS, cS.n)]
            elif ek == "g" and inc is cGC:
                waits = list(waits) + [(cGC, cGC.n)]
            wl = _coal(waits)
            if inc is not None:
                inc.n += 1

            def fn(eng, _w=wl, _b=body, _i=inc):
                for (c, v) in _w:
                    eng.wait_ge(c.sem, v)
                rec = _Rec(eng)
                _b(rec)
                if _i is not None:
                    rec.last.then_inc(_i.sem, 1)

            ops.append((ek, fn))

        def dma(ek, ctr, dst, src, waits=()):
            wl = _coal(waits)
            ctr.n += 16

            def fn(eng, _w=wl, _d=dst, _s=src, _c=ctr):
                for (c, v) in _w:
                    eng.wait_ge(c.sem, v)
                eng.dma_start(_d, _s).then_inc(_c.sem, 16)

            ops.append((ek, fn))

        bankw = [[] for _ in range(8)]

        def tjob(banks, waits, body):
            w = list(waits)
            for b in banks:
                w += bankw[b]
            job("t", w, body, cT)
            for b in banks:
                bankw[b] = []
            return cT.n

        def bread(b, ctr):
            bankw[b].append((ctr, ctr.n))

        xs_free = [[] for _ in range(4)]
        xs_i = [0]

        def xload(ek, src, extra=()):
            sl = xs_i[0] % 4
            xs_i[0] += 1
            fam = cX if ek == "g" else cXB
            dma(ek, fam[sl], xs[sl][:, :], src,
                waits=tuple(xs_free[sl]) + tuple(extra))
            xs_free[sl] = []
            return sl, (fam[sl], fam[sl].n)

        dma("sy", cW, idb[:, :], id_bf[:, :])
        dma("sy", cW, idf[:, :], id_f32[:, :])
        dma("sy", cW, msk[:, :], mask_bf[:, :])
        dma("sy", cW, STG[0:2, 0:1024], cvec[:, :])
        dma("sy", cW, outst[0][0:8, 0:128], bq.ap().rearrange("(a b) -> a b", a=8))
        dma("sy", cW, gate_sb[:, :], gate16[:, :])
        w_const = cW.n

        dma("g", cBV, WPJ[0:1, BV0:BV0 + 1024], bv.ap()[None, :])
        dma("g", cFB, fb_bf[:, :], fb[:, :])

        def konst(e):
            e.memset(onesb[:, :], 1.0)
            e.memset(kon[:, 0:1], 1e-5)
            e.memset(kon[:, 1:2], 1.0 / 32.0)
        job("v", [], konst, cV)
        v_ones = cV.n

        job("s", [(cW, w_const)],
            lambda e: e.activation(STG[0:2, 1024:2048], STG[0:2, 0:1024], AF.Silu),
            cS)
        s_silu = cS.n
        for c in range(DC):
            t_n = tjob([4], [(cS, s_silu)],
                       lambda e, c=c: e.matmul(
                           pb[4][0:128, 0:2],
                           STG[0:2, 1024 + c * 128:1024 + (c + 1) * 128],
                           idf[0:2, 0:2], is_transpose=True,
                           start=True, stop=True))
            def scev(e, c=c):
                e.tensor_copy(scT[:, 2 * c:2 * c + 2], pb[4][0:128, 0:2])
                e.tensor_copy(scT32[:, 2 * c:2 * c + 2], pb[4][0:128, 0:2])

            job("v", [(cT, t_n)], scev, cV)
            bread(4, cV)
        v_sct = cV.n
        t_sct = cT.n

        wm_rd = [[(cS, s_silu)], [(cT, t_sct)]]
        for ci in range(32):
            c, k = ci // 4, ci % 4
            srcc = (0, 1024, 3072, 4096)[k]
            sl = ci % 2
            stg = STG[:, sl * 1024:(sl + 1) * 1024]
            dma("sy" if sl == 0 else "sc", cST[sl], stg,
                wmod[c * 128:(c + 1) * 128, srcc:srcc + 1024],
                waits=wm_rd[sl])
            w_ld = (cST[sl], cST[sl].n)
            dst = KVQ[:, c * 4096 + k * 1024:c * 4096 + (k + 1) * 1024]
            if ci % 2 == 0:
                job("s", [w_ld],
                    lambda e, stg=stg, dst=dst: e.activation(
                        dst, stg, AF.Identity), cS)
                wm_rd[sl] = [(cS, cS.n)]
            else:
                job("v", [w_ld],
                    lambda e, stg=stg, dst=dst: e.tensor_copy(dst, stg), cV)
                wm_rd[sl] = [(cV, cV.n)]
        wm_done = [(cS, cS.n), (cV, cV.n)]

        t_n = tjob([5], [(cW, w_const)],
                   lambda e: e.matmul(pb[5][0:128, 0:8], outst[0][0:8, 0:128],
                                      idf[0:8, 0:8], is_transpose=True,
                                      start=True, stop=True))
        job("s", [(cT, t_n), (cV, v_ones)],
            lambda e: e.activation(bq32[:, :], pb[5][0:128, 0:8], AF.Identity,
                                   scale=kon[:, 1:2]), cS)
        bread(5, cS)
        dma("sy", cW, outst[0][0:8, 0:128],
            bk.ap().rearrange("(a b) -> a b", a=8), waits=[(cT, t_n)])
        w_bk = cW.n
        t_n = tjob([5], [(cW, w_bk)],
                   lambda e: e.matmul(pb[5][0:128, 0:8], outst[0][0:8, 0:128],
                                      idf[0:8, 0:8], is_transpose=True,
                                      start=True, stop=True))
        job("v", [(cT, t_n)],
            lambda e: e.tensor_copy(bkT[:, :], pb[5][0:128, 0:8]), cV)
        bread(5, cV)
        v_bk = cV.n
        outst_prologue_T = cT.n

        mrow_free = [[], []]
        for jgi, jg in enumerate((0, 1, 2, 3, 6, 7, 8, 9)):
            bki = jgi % 2
            mbase = jg * 512 if jg < 4 else 2048 + (jg - 6) * 512

            def mmod(e, mbase=mbase, bki=bki):
                for c in range(DC):
                    e.matmul(pb[bki][0:2, 0:512], scT[:, 2 * c:2 * c + 2],
                             KVQ[:, c * 4096 + mbase:c * 4096 + mbase + 512],
                             start=(c == 0), stop=(c == DC - 1))

            t_n = tjob([bki], wm_done + [(cV, v_sct)], mmod)
            mr = jgi % 2
            job("v", [(cT, t_n)] + mrow_free[mr],
                lambda e, mr=mr, bki=bki: e.tensor_copy(
                    STG[0:2, mr * 512:(mr + 1) * 512], pb[bki][0:2, 0:512]), cV)
            bread(bki, cV)
            v_row = cV.n
            for ui in range(4):
                j0 = jg * 512 + ui * 128
                which, cc0 = j0 // 1024, (j0 % 1024) // 128
                col = (which * 8 + cc0) * 2
                t_n = tjob([2], [(cV, v_row)],
                           lambda e, mr=mr, ui=ui: e.matmul(
                               pb[2][0:128, 0:2],
                               STG[0:2, mr * 512 + ui * 128:
                                   mr * 512 + (ui + 1) * 128],
                               idf[0:2, 0:2], is_transpose=True,
                               start=True, stop=True))
                job("v", [(cT, t_n)],
                    lambda e, col=col: e.tensor_copy(modT[:, col:col + 2],
                                                     pb[2][0:128, 0:2]), cV)
                bread(2, cV)
            mrow_free[mr] = [(cV, cV.n)]
        v_mod = cV.n

        early_x = [xload("g", xin[0, 0:128, :]), xload("g", xin[0, 128:256, :])]

        wst_free = [[], []]
        half_i = [0]
        for wsrc, wdst in ((wq, WQT0), (wk, WKT0), (wv, WVT0)):
            for hf in range(2):
                ws = half_i[0] % 4
                w2 = half_i[0] % 2
                half_i[0] += 1
                stga, base = (WPJ, WST0) if w2 == 0 else (KVQ, Q0)
                dma("g", cWS[ws],
                    stga[:, base:base + 4096],
                    bass.AP(wsrc, hf * 512 * 1024,
                            [[1024, 128], [131072, 4], [1, 1024]]),
                    waits=wst_free[w2])
                wst_free[w2] = []
                g_ld0 = (cWS[ws], cWS[ws].n)
                for f4 in range(4):
                    f = hf * 4 + f4
                    g_ld = g_ld0
                    for hb in range(2):

                        def trb(e, sl=f4, hb=hb, stga=stga, base=base):
                            for k4 in range(4):
                                c = hb * 4 + k4
                                e.matmul(
                                    pbh[hb][0:128, k4 * 128:(k4 + 1) * 128],
                                    stga[:, base + sl * 1024 + c * 128:
                                         base + sl * 1024 + (c + 1) * 128],
                                    idb[:, :], is_transpose=True,
                                    start=(k4 == 0), stop=(k4 == 3))

                        t_n = tjob([6 + hb], [g_ld], trb)
                        ek = "v" if hb == 0 else "s"
                        cc = cV if ek == "v" else cS

                        if ek == "s":
                            def ev(e, wdst=wdst, f=f, hb=hb):
                                for k4 in range(4):
                                    c = hb * 4 + k4
                                    e.copy(
                                        WPJ[:, wdst + c * 1024 + f * 128:
                                            wdst + c * 1024 + (f + 1) * 128],
                                        pbh[hb][0:128,
                                                k4 * 128:(k4 + 1) * 128])
                        else:
                            def ev(e, wdst=wdst, f=f, hb=hb):
                                for k4 in range(4):
                                    c = hb * 4 + k4
                                    e.tensor_copy(
                                        WPJ[:, wdst + c * 1024 + f * 128:
                                            wdst + c * 1024 + (f + 1) * 128],
                                        pbh[hb][0:128,
                                                k4 * 128:(k4 + 1) * 128])

                        job(ek, [(cT, t_n)], ev, cc)
                        bread(6 + hb, cc)
                    wst_free[w2].append((cT, cT.n))
        t_guard = cT.n
        stgR = [[(cT, t_guard)], [(cT, t_guard)]]
        row_guard = [[]]
        for which in (2, 5):
            mm_w = []
            for c in range(DC):
                sl = c % 2
                dma("sy" if sl == 0 else "sc", cST[sl],
                    STG[:, sl * 1024:(sl + 1) * 1024],
                    wmod[c * 128:(c + 1) * 128, which * 1024:(which + 1) * 1024],
                    waits=stgR[sl])
                w_ld = (cST[sl], cST[sl].n)

                def mref(e, c=c, sl=sl):
                    e.matmul(pb[0][0:2, 0:512], scT32[:, 2 * c:2 * c + 2],
                             STG[:, sl * 1024:sl * 1024 + 512],
                             start=(c == 0), stop=(c == DC - 1))
                    e.matmul(pb[1][0:2, 0:512], scT32[:, 2 * c:2 * c + 2],
                             STG[:, sl * 1024 + 512:(sl + 1) * 1024],
                             start=(c == 0), stop=(c == DC - 1))

                bw = [w_ld, (cV, v_sct)]
                if c == 0:
                    bw += bankw[0] + bankw[1]
                    bankw[0] = []
                    bankw[1] = []
                job("t", bw, mref, cT)
                stgR[sl] = [(cT, cT.n)]
            t_mm = cT.n
            for half in range(2):
                job("v", [(cT, t_mm)] + row_guard[0], 
                    lambda e, half=half: e.tensor_copy(
                        outst[1][0:2, 0:512], pb[half][0:2, 0:512]), cV)
                bread(half, cV)
                v_row = cV.n
                for ui in range(4):
                    cc0 = half * 4 + ui
                    col = (which * 8 + cc0) * 2
                    t_n = tjob([2], [(cV, v_row)],
                               lambda e, ui=ui: e.matmul(
                                   pb[2][0:128, 0:2],
                                   outst[1][0:2, ui * 128:(ui + 1) * 128],
                                   idf[0:2, 0:2], is_transpose=True,
                                   start=True, stop=True))
                    job("v", [(cT, t_n)],
                        lambda e, col=col: e.tensor_copy(
                            modT[:, col:col + 2], pb[2][0:128, 0:2]), cV)
                    bread(2, cV)
                row_guard[0] = [(cT, cT.n)]
        v_modref = cV.n

        def derive(e):
            e.tensor_scalar(caP1[:, :], modT[:, 16:32], 1.0, None, ALU.add)
            e.tensor_copy(saT[:, :], modT[:, 0:16])
            e.tensor_scalar(cbP1[:, :], modT[:, 64:80], 1.0, None, ALU.add)
            e.tensor_copy(sbT[:, :], modT[:, 48:64])

        job("v", [(cV, v_mod)], derive, cV)

        def clampg(base, dst):
            def f(e):
                for h8 in range(2):
                    src0 = modT[:, base + h8 * 8:base + h8 * 8 + 8]
                    d8 = dst[:, h8 * 8:h8 * 8 + 8]
                    e.tensor_scalar(d8, src0, 0.25, None, ALU.max)
                    e.tensor_scalar(scpj[:, :], src0, -0.25, None, ALU.min)
                    e.tensor_single_scalar(pabs[:, :], src0, 0.0, ALU.is_ge)
                    e.drain()
                    e.tensor_sub(d8, d8, scpj[:, :])
                    e.drain()
                    e.tensor_mul(d8, d8, pabs[:, :])
                    e.drain()
                    e.tensor_add(d8, d8, scpj[:, :])
                    e.drain()
            return f

        job("v", [(cV, v_mod), (cV, v_modref)], clampg(32, gaTc), cV)
        job("v", [], clampg(80, gbTc), cV)
        v_gates = cV.n

        def bcast_gate(src_t, arena, dst_base, first_waits):
            job("v", list(first_waits) + [(cV, v_gates), (cV, v_ones)],
                lambda e, src_t=src_t: e.tensor_copy(scT[:, :], src_t[:, :]),
                cV)
            v_cp = cV.n
            v1s = {}
            for col in range(16):
                pp = col % 2
                t1 = tjob([6 + pp], [(cV, v_cp)],
                          lambda e, col=col, pp=pp: e.matmul(
                              pbh[pp][0:1, 0:128], scT[:, col:col + 1],
                              idb[:, :], is_transpose=True,
                              start=True, stop=True))
                job("v", [(cT, t1)] + list(first_waits),
                    lambda e, pp=pp, col=col: e.tensor_copy(
                        Pb[0:1, col * 128:(col + 1) * 128],
                        pbh[pp][0:1, 0:128]), cV)
                bread(6 + pp, cV)
                v1s[col] = cV.n
            for q in range(4):
                b = q // 2
                cbase = (q % 2) * 4
                bk = 3 + (q % 2)
                cols = [(cbase + j) * 2 + b for j in range(4)]

                def mbc(e, cols=cols, bk=bk):
                    for j, col in enumerate(cols):
                        e.matmul(pb[bk][:, j * 128:(j + 1) * 128],
                                 onesb[:, :],
                                 Pb[0:1, col * 128:(col + 1) * 128],
                                 start=(j == 0), stop=(j == 3))

                t2 = tjob([bk], [(cV, v1s[c]) for c in cols], mbc)
                dst = arena[:, dst_base + b * 1024 + cbase * 128:
                            dst_base + b * 1024 + (cbase + 4) * 128]
                if q % 2 == 0:
                    job("v", [(cT, t2)],
                        lambda e, dst=dst, bk=bk:
                        e.tensor_copy(dst, pb[bk][:, 0:512]), cV)
                    bread(bk, cV)
                else:
                    job("s", [(cT, t2)],
                        lambda e, dst=dst, bk=bk:
                        e.copy(dst, pb[bk][:, 0:512]), cS)
                    bread(bk, cS)
            return [(cV, cV.n), (cS, cS.n)]

        gab_done = bcast_gate(gaTc, WPJ, GA0, [])

        v_derive = v_gates

        xh_free = [[], []]
        stt_free = [[], []]

        def ln_chain(sl, ldw, r):
            if isinstance(ldw, tuple):
                ldw = [ldw]
            st = stt[r]
            job("v", list(ldw) + stt_free[r],
                lambda e, st=st, sl=sl: e.reduce_sum(st[:, 0:1], xs[sl][:, :],
                                                     AX.X), cV)
            job("s", list(ldw) + xh_free[r] + stt_free[r],
                lambda e, st=st, sl=sl, r=r: e.activation(
                    xh[r][:, :], xs[sl][:, :], AF.Square, accum_out=st[:, 1:2]),
                cS)
            s_sq = cS.n

            def statf(e, st=st):
                e.tensor_scalar(st[:, 2:3], st[:, 0:1], 1.0 / 1024.0, None,
                                ALU.mult)
                e.drain()
                e.tensor_mul(st[:, 3:4], st[:, 2:3], st[:, 2:3])
                e.drain()
                e.tensor_scalar(st[:, 4:5], st[:, 1:2], 1.0 / 1024.0, st[:, 3:4],
                                ALU.mult, ALU.subtract)

            job("v", [(cS, s_sq)], statf, cV)
            job("s", [(cV, cV.n)],
                lambda e, st=st: e.activation(st[:, 5:6], st[:, 4:5], AF.Sqrt,
                                              bias=kon[:, 0:1]), cS)
            s_rt = cS.n

            def vxh(e, st=st, sl=sl, r=r):
                e.reciprocal(st[:, 6:7], st[:, 5:6])
                e.drain()
                e.tensor_scalar(xh[r][:, :], xs[sl][:, :], st[:, 2:3],
                                st[:, 6:7], ALU.subtract, ALU.mult)

            job("v", [(cS, s_rt)], vxh, cV)
            v_xh = cV.n
            xs_free[sl].append((cV, v_xh))
            stt_free[r] = [(cV, v_xh)]
            return (cV, v_xh)

        def ln_tr(vxh_w, r, scale_t, bias_t, smcol, dst_fn):
            hb = r % 2

            def trb(e, hb=hb, r=r):
                for c in range(8):
                    e.matmul(pbh[hb][0:128, c * 128:(c + 1) * 128],
                             xh[r][:, c * 128:(c + 1) * 128],
                             idb[:, :], is_transpose=True,
                             start=(c == 0), stop=(c == 7))

            t_n = tjob([6 + hb], [vxh_w], trb)
            ek = "s" if r % 2 == 0 else "v"
            cc = cS if ek == "s" else cV

            if ek == "s":
                def ev(e, hb=hb, smcol=smcol, scale_t=scale_t,
                       bias_t=bias_t):
                    for c in range(8):
                        col = c * 2 + smcol
                        e.activation(dst_fn(c),
                                     pbh[hb][0:128, c * 128:(c + 1) * 128],
                                     AF.Identity,
                                     scale=scale_t[:, col:col + 1],
                                     bias=bias_t[:, col:col + 1])
            else:
                def ev(e, hb=hb, smcol=smcol, scale_t=scale_t,
                       bias_t=bias_t):
                    for c in range(8):
                        col = c * 2 + smcol
                        e.tensor_scalar(dst_fn(c),
                                        pbh[hb][0:128,
                                                c * 128:(c + 1) * 128],
                                        scale_t[:, col:col + 1],
                                        bias_t[:, col:col + 1],
                                        ALU.mult, ALU.add)

            job(ek, [(cT, t_n), (cV, v_derive)], ev, cc)
            bread(6 + hb, cc)
            xh_free[r] = [(cT, t_n)]

        stg_free = [[(cT, cT.n)], [(cT, cT.n)]]
        we_free = [[]]
        we1_war = [[]]
        wef_free = [[], [], []]
        wef_ready = {}
        ts_free = [[], [], [], []]
        pabs_free = [[]]
        wefp_free = [[], []]

        tern_i = [0]

        def tern_transposes(wef_ap, dst_fn, first_waits):
            hb = tern_i[0] % 2
            tern_i[0] += 1

            def trb(e, hb=hb, wef_ap=wef_ap):
                for c in range(8):
                    e.matmul(pbh[hb][0:128, c * 128:(c + 1) * 128],
                             wef_ap[:, c * 128:(c + 1) * 128],
                             idb[:, :], is_transpose=True,
                             start=(c == 0), stop=(c == 7))

            t_n = tjob([6 + hb], list(first_waits), trb)
            ek = "s" if hb == 0 else "v"
            cc = cS if ek == "s" else cV

            if ek == "s":
                def ev(e, hb=hb, dst_fn=dst_fn):
                    for c in range(8):
                        e.copy(dst_fn(c),
                               pbh[hb][0:128, c * 128:(c + 1) * 128])
            else:
                def ev(e, hb=hb, dst_fn=dst_fn):
                    for c in range(8):
                        e.tensor_copy(dst_fn(c),
                                      pbh[hb][0:128,
                                              c * 128:(c + 1) * 128])

            job(ek, [(cT, t_n)], ev, cc)
            bread(6 + hb, cc)
            return t_n

        WEFW = [0, 1024, 3072]

        def wfc_wef(jc, post_attn):
            sl = jc % 2
            stg = STG[:, sl * 1024:(sl + 1) * 1024]
            dma("sy", cST[sl], stg,
                wfc[jc * 128:(jc + 1) * 128, :],
                waits=stg_free[sl])
            w_ld = (cST[sl], cST[sl].n)
            ts_ = tsc[jc % 4]
            dumpA = C[:, 2048:3072]
            wp = jc % 3
            wefb = C[:, WEFW[wp]:WEFW[wp] + 1024]
            dw = [(cT, attn_T_done[1])] if post_attn else []
            job("s", [w_ld] + dw + ts_free[jc % 4],
                lambda e, stg=stg, dumpA=dumpA, ts_=ts_: e.activation(
                    dumpA, stg, AF.Abs, accum_out=ts_[:, 0:1]), cS)
            s_ab = cS.n

            def scf(e, ts_=ts_):
                e.tensor_scalar(ts_[:, 1:2], ts_[:, 0:1], 1.0 / 1024.0, 1e-5,
                                ALU.mult, ALU.max)
                e.drain()
                e.tensor_scalar(ts_[:, 2:3], ts_[:, 1:2], 0.5, None, ALU.mult)
                e.tensor_scalar(ts_[:, 0:1], ts_[:, 1:2], -1.0, None, ALU.mult)
                e.drain()
                e.tensor_scalar(ts_[:, 3:4], ts_[:, 2:3], -1.0, None, ALU.mult)

            job("v", [(cS, s_ab)], scf, cV)
            job("v", wef_free[wp],
                lambda e, stg=stg, ts_=ts_, wefb=wefb: e.tensor_scalar(
                    wefb, stg, ts_[:, 2:3], ts_[:, 1:2], ALU.is_ge,
                    ALU.mult), cV)
            job("v", [],
                lambda e, stg=stg, ts_=ts_: e.tensor_scalar(
                    we1[:, :], stg, ts_[:, 3:4], ts_[:, 0:1], ALU.is_le,
                    ALU.mult), cV)
            job("v", [],
                lambda e, wefb=wefb: e.tensor_add(wefb, wefb, we1[:, :]), cV)
            v_wef = cV.n
            stg_free[sl] = [(cV, v_wef)]
            ts_free[jc % 4] = [(cV, v_wef)]
            we_free[0] = [(cV, v_wef)]
            wef_ready[jc] = (cV, v_wef)

        def wfc_tr(jc, post_attn):
            wp = jc % 3
            wefb = C[:, WEFW[wp]:WEFW[wp] + 1024]
            dw = [(cT, attn_T_done[1])] if post_attn else []
            last_tr = tern_transposes(
                wefb,
                (lambda c, jc=jc: WPJ[:, jc * 1024 + c * 128:
                                      jc * 1024 + (c + 1) * 128]),
                [wef_ready[jc]] + dw)
            wef_free[wp] = [(cT, last_tr)]

        def wfc_chunk(jc, post_attn):
            wfc_wef(jc, post_attn)
            wfc_tr(jc, post_attn)

        def wproj_p1_chunk(q_i):
            ec, qq = q_i // 4, q_i % 4
            sl = q_i % 2
            stg = STG[:, sl * 1024:(sl + 1) * 1024]
            dma("sy" if sl == 0 else "sc", cST[sl], stg,
                wpj[ec * 128:(ec + 1) * 128, qq * 1024:(qq + 1) * 1024],
                waits=stg_free[sl])
            w_ld = (cST[sl], cST[sl].n)
            job("s", [w_ld, (cV, v_gates)] + list(kqv_ready[0]) +
                (pabs_free[0] if qq == 0 else []),
                lambda e, stg=stg, qq=qq: e.activation(
                    C[:, 2048:3072], stg, AF.Abs,
                    accum_out=pabs[:, qq:qq + 1]), cS)
            stg_free[sl] = [(cS, cS.n)]
            if qq == 3:
                def comb(e, ec=ec):
                    e.reduce_sum(pabs[:, 4:5], pabs[:, 0:4], AX.X)
                    e.drain()
                    e.tensor_scalar(scpj[:, ec:ec + 1], pabs[:, 4:5],
                                    1.0 / 4096.0, 1e-5, ALU.mult, ALU.max)

                job("v", [(cS, cS.n)], comb, cV)
                pabs_free[0] = [(cV, cV.n)]

        p2_ready = {}

        def wproj_p2_wef(q_i):
            ec, qq = q_i // 4, q_i % 4
            sl = q_i % 2
            stg = STG[:, sl * 1024:(sl + 1) * 1024]
            dma("sc", cST[sl], stg,
                wpj[ec * 128:(ec + 1) * 128, qq * 1024:(qq + 1) * 1024],
                waits=stg_free[sl])
            w_ld = (cST[sl], cST[sl].n)
            ts_ = tsc[q_i % 2]
            ts_free[q_i % 2] = [(cV, cV.n)]

            def scf(e, ts_=ts_, ec=ec):
                e.tensor_scalar(ts_[:, 2:3], scpj[:, ec:ec + 1], 0.5, None,
                                ALU.mult)
                e.tensor_scalar(ts_[:, 0:1], scpj[:, ec:ec + 1], -1.0, None,
                                ALU.mult)
                e.tensor_copy(ts_[:, 1:2], scpj[:, ec:ec + 1])
                e.drain()
                e.tensor_scalar(ts_[:, 3:4], ts_[:, 2:3], -1.0, None, ALU.mult)

            job("v", [w_ld], scf, cV)
            wp = q_i % 2
            wefb = Pb[:, wp * 1024:(wp + 1) * 1024]
            extra_pb = ([(cT, attn_T_done[1])] + gbb_done) if q_i < 2 else []
            job("v", wefp_free[wp] + extra_pb,
                lambda e, stg=stg, ts_=ts_, wefb=wefb: e.tensor_scalar(
                    wefb, stg, ts_[:, 2:3], ts_[:, 1:2], ALU.is_ge,
                    ALU.mult), cV)
            job("v", we1_war[0],
                lambda e, stg=stg, ts_=ts_: e.tensor_scalar(
                    we1[:, :], stg, ts_[:, 3:4], ts_[:, 0:1], ALU.is_le,
                    ALU.mult), cV)
            job("v", [],
                lambda e, wefb=wefb: e.tensor_add(wefb, wefb, we1[:, :]), cV)
            v_wef = cV.n
            stg_free[sl] = [(cV, v_wef)]
            p2_ready[q_i] = (cV, v_wef)

        def wproj_p2_tr(q_i):
            ec, qq = q_i // 4, q_i % 4
            wp = q_i % 2
            wefb = Pb[:, wp * 1024:(wp + 1) * 1024]
            last_tr = tern_transposes(
                wefb,
                (lambda js, qq=qq, ec=ec: KVQ[:, WPJT0 + (qq * 8 + js) * 1024 +
                                              ec * 128:
                                              WPJT0 + (qq * 8 + js) * 1024 +
                                              (ec + 1) * 128]),
                [p2_ready[q_i]])
            wefp_free[wp] = [(cT, last_tr)]

        x1r_free = [[] for _ in range(2)]
        kqv_ready = [None, None]
        attn_T_done = [0, 0]

        for s in range(SEQ):
            ev_k = ev_q = ev_v = (cV, 0)

            ch_state = {}

            def a_chain(t, s=s, ch_state=ch_state):
                if s == 0 and t < 2:
                    sl, ldw = early_x[t]
                else:
                    sl, ldw = xload("g", xin[s, t * 128:(t + 1) * 128, :])
                ch_state[t] = ln_chain(sl, ldw, t % 2)

            def a_tr(t, s=s, ch_state=ch_state):
                ln_tr(ch_state[t], t % 2, caP1, saT, s,
                      (lambda c, ti=t % 4: C[:, c * 512 + ti * 128:
                                             c * 512 + ti * 128 + 128]))

            a_chain(0)
            a_chain(1)
            for g in range(NG):
                a_tr(4 * g)
                a_tr(4 * g + 1)
                a_chain(4 * g + 2)
                a_chain(4 * g + 3)
                a_tr(4 * g + 2)
                a_tr(4 * g + 3)
                h_ready = [(cS, cS.n), (cV, cV.n)]
                if g < NG - 1:
                    a_chain(4 * g + 4)
                    a_chain(4 * g + 5)
                for f in range(DC):
                    bki = f % 2

                    def mmk(e, f=f, g=g, bki=bki):
                        for c in range(DC):
                            e.matmul(pb[bki][:, 0:512],
                                     WPJ[:, WKT0 + c * 1024 + f * 128:
                                         WKT0 + c * 1024 + (f + 1) * 128],
                                     C[:, c * 512:(c + 1) * 512],
                                     start=(c == 0), stop=(c == DC - 1))

                    t_n = tjob([bki], h_ready if f == 0 else [], mmk)
                    job("s", [(cT, t_n), (cV, v_bk)],
                        lambda e, f=f, g=g, bki=bki: e.activation(
                            KVQ[:, K0 + f * 2048 + g * 512:
                                K0 + f * 2048 + g * 512 + 512],
                            pb[bki][:, 0:512], AF.Identity, bias=bkT[:, f:f + 1]),
                        cS)
                    bread(bki, cS)
                ev_k = (cS, cS.n)
                for f in range(DC):
                    bki = 2 + (f % 2)

                    def mmq(e, f=f, g=g, bki=bki):
                        for c in range(DC):
                            e.matmul(pb[bki][:, 0:512],
                                     WPJ[:, WQT0 + c * 1024 + f * 128:
                                         WQT0 + c * 1024 + (f + 1) * 128],
                                     C[:, c * 512:(c + 1) * 512],
                                     start=(c == 0), stop=(c == DC - 1))

                    t_n = tjob([bki], [], mmq)
                    job("s", [(cT, t_n)],
                        lambda e, f=f, g=g, bki=bki: e.activation(
                            KVQ[:, Q0 + f * 2048 + g * 512:
                                Q0 + f * 2048 + g * 512 + 512],
                            pb[bki][:, 0:512], AF.Identity, scale=kon[:, 1:2],
                            bias=bq32[:, f:f + 1]), cS)
                    bread(bki, cS)
                ev_q = (cS, cS.n)
                for ui in range(4):
                    tabs = g * 4 + ui
                    for hf in range(2):
                        bki = 4 + ((ui * 2 + hf) % 2)

                        def mmv(e, ui=ui, hf=hf, bki=bki):
                            e.matmul(pb[bki][:, 0:512], onesb[:, :],
                                     WPJ[0:1, BV0 + hf * 512:
                                         BV0 + (hf + 1) * 512],
                                     start=True, stop=False)
                            for c in range(DC):
                                e.matmul(pb[bki][:, 0:512],
                                         C[:, c * 512 + ui * 128:
                                           c * 512 + (ui + 1) * 128],
                                         WPJ[:, WVT0 + c * 1024 + hf * 512:
                                             WVT0 + c * 1024 + (hf + 1) * 512],
                                         start=False, stop=(c == DC - 1))

                        w0 = ([(cBV, 16), (cV, v_ones)] + gab_done) if (
                            s == 0 and g == 0 and ui == 0 and hf == 0) else []
                        t_n = tjob([bki], w0, mmv)
                        job("v", [(cT, t_n)],
                            lambda e, tabs=tabs, hf=hf, s=s, bki=bki:
                            e.tensor_tensor(
                                KVQ[:, V0 + tabs * 1024 + hf * 512:
                                    V0 + tabs * 1024 + (hf + 1) * 512],
                                pb[bki][:, 0:512],
                                WPJ[:, GA0 + s * 1024 + hf * 512:
                                    GA0 + s * 1024 + (hf + 1) * 512],
                                ALU.mult), cV)
                        bread(bki, cV)
                ev_v = (cV, cV.n)
            kqv_ready[s] = [ev_k, ev_q, ev_v]

            pt_sl_free = [[], []]
            pt_read_prev = {0: None, 1: None}
            pend = None
            wfc_jc = 0
            wfc_tr_jc = 0
            wproj_q = 0

            def emit_ptpv(st8, s=s):
                pi, p_exp_n, p_ldw, p_sl, p_ldone, p_buf = st8
                t_pv = 0
                njj = pi + 1
                ngr = (njj + 3) // 4
                for g4 in range(ngr):
                    j0 = g4 * 4
                    j1 = min(j0 + 4, njj)
                    hb = g4 % 2
                    PTs = PT

                    def trg(e, j0=j0, j1=j1, hb=hb, p_buf=p_buf):
                        for j in range(j0, j1):
                            e.matmul(pbh[hb][0:128, (j - j0) * 128:
                                             (j - j0 + 1) * 128],
                                     p_buf[:, j * 128:(j + 1) * 128],
                                     idb[:, :], is_transpose=True,
                                     start=(j == j0), stop=(j == j1 - 1))

                    t_n = tjob([6 + hb], [(cS, p_exp_n)] if g4 == 0 else [], trg)
                    if g4 % 2 == 0:
                        job("v", [(cT, t_n)] + pt_sl_free[0],
                            lambda e, j0=j0, j1=j1, hb=hb, PTs=PTs:
                            e.tensor_copy(
                                PTs[:, 0:(j1 - j0) * 128],
                                pbh[hb][0:128, 0:(j1 - j0) * 128]), cV)
                        bread(6 + hb, cV)
                        v_pt = (cV, cV.n)
                    else:
                        job("s", [(cT, t_n)] + pt_sl_free[0],
                            lambda e, j0=j0, j1=j1, hb=hb, PTs=PTs: e.copy(
                                PTs[:, 0:(j1 - j0) * 128],
                                pbh[hb][0:128, 0:(j1 - j0) * 128]), cS)
                        bread(6 + hb, cS)
                        v_pt = (cS, cS.n)

                    def mpv(e, j0=j0, j1=j1, pi=pi, PTs=PTs):
                        for j in range(j0, j1):
                            e.matmul(pb[4][:, 0:512],
                                     PTs[:, (j - j0) * 128:(j - j0 + 1) * 128],
                                     KVQ[:, V0 + j * 1024:V0 + j * 1024 + 512],
                                     start=(j == 0), stop=(j == pi))
                            e.matmul(pb[5][:, 0:512],
                                     PTs[:, (j - j0) * 128:(j - j0 + 1) * 128],
                                     KVQ[:, V0 + j * 1024 + 512:
                                         V0 + (j + 1) * 1024],
                                     start=(j == 0), stop=(j == pi))

                    t_pv = tjob([4, 5] if g4 == 0 else [], [v_pt], mpv)
                    pt_sl_free[0] = [(cT, t_pv)]
                tr_done = (cT, cT.n)
                for hf in range(2):
                    rsl = (2 * pi + hf) % 2
                    job("v", [(cT, t_pv), p_ldone, p_ldw] + x1r_free[rsl],
                        lambda e, hf=hf, rsl=rsl, pi=pi, p_sl=p_sl:
                        e.scalar_tensor_tensor(
                            x1r[rsl][:, :], pb[4 + hf][:, 0:512],
                            linv[pi % 2][:, :],
                            xs[p_sl][:, hf * 512:(hf + 1) * 512],
                            ALU.mult, ALU.add), cV)
                    bread(4 + hf, cV)
                    dma("g", cSP[rsl],
                        x1s[s, pi * 128:(pi + 1) * 128,
                            hf * 512:(hf + 1) * 512],
                        x1r[rsl][:, :], waits=[(cV, cV.n)])
                    x1r_free[rsl] = [(cSP[rsl], cSP[rsl].n)]
                xs_free[p_sl].append((cV, cV.n))
                return tr_done

            for i in range(NT):
                L = (i + 1) * 128
                nb = (L + 511) // 512
                sb0 = (i % 2) * 2 if nb <= 2 else 0
                sl, ldw = xload("g", xin[s, i * 128:(i + 1) * 128, :])

                db_, off_ = (i * 128) // 512, (i * 128) % 512
                sc_w = (list(kqv_ready[s]) + [(cW, w_const)]) if i == 0 \
                    else []
                t_scs = []
                for kb in range(nb):
                    n_kb = min(512, L - kb * 512)
                    dg = (kb == db_)

                    def mms(e, i=i, kb=kb, n_kb=n_kb, dg=dg, sb0=sb0,
                            off_=off_):
                        for f in range(DC):
                            e.matmul(pb[sb0 + kb][:, 0:n_kb],
                                     KVQ[:, Q0 + f * 2048 + i * 128:
                                         Q0 + f * 2048 + (i + 1) * 128],
                                     KVQ[:, K0 + f * 2048 + kb * 512:
                                         K0 + f * 2048 + kb * 512 + n_kb],
                                     start=(f == 0),
                                     stop=(not dg and f == DC - 1))
                        if dg:
                            e.matmul(pb[sb0 + kb][:, off_:off_ + 128],
                                     idb[:, :], msk[:, :], start=False,
                                     stop=True)

                    t_scs.append(tjob([sb0 + kb], sc_w if kb == 0 else [],
                                      mms))
                Pbi = Pb2 if (i % 2 == 1 and i != 15) else Pb
                key = 1 if Pbi is Pb2 else 0
                if pend is not None:
                    pt_read_prev[1 if pend[5] is Pb2 else 0] = emit_ptpv(pend)
                    pend = None
                pw = pt_read_prev[key]
                exp_w = [pw] if pw else []
                for kb in range(nb):
                    n_kb = min(512, L - kb * 512)
                    job("s", [(cT, t_scs[kb])] + (exp_w if kb == 0 else []),
                        lambda e, kb=kb, n_kb=n_kb, sb0=sb0, Pbi=Pbi,
                        lo=(i % 2) * 4:
                        e.activation(
                            Pbi[:, kb * 512:kb * 512 + n_kb],
                            pb[sb0 + kb][:, 0:n_kb], AF.Exp,
                            accum_out=lsum[:, lo + kb:lo + kb + 1]), cS)
                    bread(sb0 + kb, cS)
                exp_n = cS.n
                if s == 1 and i >= 1:
                    if wfc_tr_jc < wfc_jc - 1:
                        wfc_tr(wfc_tr_jc, False)
                        wfc_tr_jc += 1
                    if wfc_jc < 28:
                        wfc_wef(wfc_jc, False)
                        wfc_jc += 1

                def vl(e, i=i, nb=nb):
                    lo = (i % 2) * 4
                    e.reduce_sum(lred[i % 2][:, :], lsum[:, lo:lo + nb], AX.X)
                    e.drain()
                    e.reciprocal(linv[i % 2][:, :], lred[i % 2][:, :])

                job("v", [(cS, exp_n)], vl, cV)
                pend = (i, exp_n, ldw, sl, (cV, cV.n), Pbi)

                if s == 1 and i >= 1:
                    if wfc_tr_jc < wfc_jc - 1:
                        wfc_tr(wfc_tr_jc, False)
                        wfc_tr_jc += 1
                    if wfc_jc < 28:
                        wfc_wef(wfc_jc, False)
                        wfc_jc += 1
                if s == 0 and wproj_q < 32:
                    for _ in range(2):
                        if wproj_q < 32:
                            wproj_p1_chunk(wproj_q)
                            wproj_q += 1
            emit_ptpv(pend)
            attn_T_done[s] = cT.n
            if s == 1:
                while wfc_jc < 28:
                    wfc_wef(wfc_jc, False)
                    wfc_jc += 1
                while wfc_tr_jc < 28:
                    wfc_tr(wfc_tr_jc, False)
                    wfc_tr_jc += 1

        for jc in (28, 29, 30):
            wfc_wef(jc, True)
        wfc_tr(28, True)
        wfc_wef(31, True)
        for jc in (29, 30, 31):
            wfc_tr(jc, True)
        wfc_all = [(cS, cS.n), (cV, cV.n)]
        we_free[0] = [(cV, cV.n)]
        sp_final = [(c, c.n) for c in cSP]

        t_n = tjob([6], [(cFB, 16), (cT, attn_T_done[1])],
                   lambda e: e.matmul(pbh[0][0:16, 0:16], fb_bf[:, :],
                                      idb[0:16, 0:16], is_transpose=True,
                                      start=True, stop=True))
        job("v", [(cT, t_n)],
            lambda e: e.tensor_copy(fbT[:, :], pbh[0][0:16, 0:16]), cV)
        bread(6, cV)
        v_fbt = cV.n
        dma("g", cVL, Pb2[0:16, 0:1024], vlr[:, :],
            waits=[(cT, attn_T_done[1])])
        dma("g", cU, we1[:, 0:512],
            bass.AP(uu, 0, [[16, 128], [2048, 32], [1, 16]]), waits=we_free[0])

        def mfbv(e):
            e.matmul(pb[0][0:16, 0:512], fbT[:, :], Pb2[0:16, 0:512],
                     start=True, stop=True)
            e.matmul(pb[1][0:16, 0:512], fbT[:, :], Pb2[0:16, 512:1024],
                     start=True, stop=True)

        t_n = tjob([0, 1], [(cV, v_fbt), (cVL, 16)], mfbv)

        def efbv(e):
            e.copy(C[0:16, 3072:3584], pb[0][0:16, 0:512])
            e.copy(C[0:16, 3584:4096], pb[1][0:16, 0:512])

        job("s", [(cT, t_n)], efbv, cS)
        bread(0, cS)
        bread(1, cS)
        s_fbv = cS.n
        for gidx in range(2):
            hb = gidx % 2

            def trv(e, gidx=gidx, hb=hb):
                for k in range(4):
                    c = gidx * 4 + k
                    e.matmul(pbh[hb][0:128, k * 16:(k + 1) * 16],
                             C[0:16, 3072 + c * 128:3072 + (c + 1) * 128],
                             idb[0:16, 0:16], is_transpose=True,
                             start=(k == 0), stop=(k == 3))

            t_n = tjob([6 + hb], [(cS, s_fbv)], trv)
            job("v", [(cT, t_n)],
                lambda e, gidx=gidx, hb=hb: e.tensor_copy(
                    vlrT[:, gidx * 64:(gidx + 1) * 64],
                    pbh[hb][0:128, 0:64]), cV)
            bread(6 + hb, cV)
        for g8 in range(8):
            hb = g8 % 2

            def tru(e, g8=g8, hb=hb):
                for k in range(4):
                    jc = g8 * 4 + k
                    e.matmul(pbh[hb][0:16, k * 128:(k + 1) * 128],
                             we1[:, jc * 16:(jc + 1) * 16],
                             idb[:, :], is_transpose=True,
                             start=(k == 0), stop=(k == 3))

            t_n = tjob([6 + hb], [(cU, 16)], tru)
            job("v", [(cT, t_n)],
                lambda e, g8=g8, hb=hb: e.tensor_copy(
                    C[0:16, g8 * 512:(g8 + 1) * 512],
                    pbh[hb][0:16, 0:512]), cV)
            bread(6 + hb, cV)
        we1_war[0] = [(cT, cT.n)]
        v_lrsetup = cV.n

        gbb_done = bcast_gate(gbTc, KVQ, GB0, [(cT, attn_T_done[1])])

        r2_free = [[]]
        relu_free = [[], []]
        tmp_free = [[], []]
        outst_free = [[(cT, outst_prologue_T)], []]
        wproj_started = [False]
        wproj_all = [None]
        t12_free = [[]]

        b_state = {}
        b_slots = {}

        def b_load(tg):
            s2 = tg // NT
            tt = tg % NT
            sl = xs_i[0] % 4
            xs_i[0] += 1
            w = tuple(xs_free[sl]) + tuple(sp_final)
            dma("g", cX[sl], xs[sl][:, 0:512],
                x1s[s2, tt * 128:(tt + 1) * 128, 0:512], waits=w)
            dma("sy", cXB[sl], xs[sl][:, 512:1024],
                x1s[s2, tt * 128:(tt + 1) * 128, 512:1024], waits=w)
            xs_free[sl] = []
            b_slots[tg] = (sl, [(cX[sl], cX[sl].n), (cXB[sl], cXB[sl].n)])

        def b_ln(tg):
            sl, ldw = b_slots[tg]
            b_state[tg] = ln_chain(sl, ldw, tg % 2)

        def b_chain(tg):
            b_load(tg)
            b_ln(tg)

        def b_tr(tg):
            s2 = tg // NT
            b2 = (tg // 2) % 2
            k2 = tg % 2
            ln_tr(b_state[tg], tg % 2, cbP1, sbT, s2,
                  (lambda c, k2=k2, b2=b2:
                   KVQ[:, H20 + b2 * 2048 + c * 256 + k2 * 128:
                       H20 + b2 * 2048 + c * 256 + k2 * 128 + 128]))

        b_chain(0)
        b_chain(1)
        b_tr(0)
        b_tr(1)
        h2w_arr = {0: [(cS, cS.n), (cV, cV.n)]}
        b_chain(2)
        b_chain(3)
        t2g_done_arr = {}

        def emit_t2(gg):
            b2 = gg % 2
            T2G = 48640 + (gg % 2) * 256

            def mt2(e, b2=b2):
                for c in range(DC):
                    e.matmul(pb[2][0:16, 0:256], vlrT[:, c * 16:(c + 1) * 16],
                             KVQ[:, H20 + b2 * 2048 + c * 256:
                                 H20 + b2 * 2048 + (c + 1) * 256],
                             start=(c == 0), stop=(c == DC - 1))

            w0 = list(h2w_arr[gg]) + t12_free[0]
            if gg == 0:
                w0 += [(cV, v_lrsetup)]
            t_n = tjob([2], w0, mt2)
            job("s", [(cT, t_n), (cW, w_const)],
                lambda e, T2G=T2G: e.activation(
                    KVQ[0:16, T2G:T2G + 256], pb[2][0:16, 0:256],
                    AF.Identity, scale=gate_sb[:, :]), cS)
            bread(2, cS)
            t2g_done_arr[gg] = (cS, cS.n)
            t12_free[0] = [(cT, cT.n)]

        emit_t2(0)
        for gg in range(NFG):
            s = gg // 8
            b2 = gg % 2
            tiles = [gg * 2, gg * 2 + 1]
            tile_slots = [b_slots[2 * gg], b_slots[2 * gg + 1]]
            T2G = 48640 + (gg % 2) * 256
            t2g_done = t2g_done_arr[gg]

            fc_last_ev = None
            FCB = (0, 1, 2, 5)
            for jc in range(JC):
                bki = FCB[jc % 4]

                def mfc(e, jc=jc, b2=b2, bki=bki, T2G=T2G):
                    e.matmul(pb[bki][:, 0:256], C[0:16, jc * 128:(jc + 1) * 128],
                             KVQ[0:16, T2G:T2G + 256], start=True, stop=False)
                    for c in range(DC):
                        e.matmul(pb[bki][:, 0:256],
                                 WPJ[:, jc * 1024 + c * 128:
                                     jc * 1024 + (c + 1) * 128],
                                 KVQ[:, H20 + b2 * 2048 + c * 256:
                                     H20 + b2 * 2048 + (c + 1) * 256],
                                 start=False, stop=(c == DC - 1))

                w0 = [t2g_done] if jc == 0 else []
                if gg == 0 and jc == 0:
                    w0 += wfc_all + gbb_done
                t_n = tjob([bki], w0, mfc)
                rsl = jc % 2
                job("s", [(cT, t_n)] + relu_free[rsl],
                    lambda e, rsl=rsl, bki=bki: e.activation(
                        KVQ[:, RLB0 + rsl * 256:RLB0 + (rsl + 1) * 256],
                        pb[bki][:, 0:256], AF.Relu), cS)
                bread(bki, cS)
                s_rl = cS.n
                job("v", ([(cS, s_rl)] + r2_free[0]) if jc == 0 else
                    [(cS, s_rl)],
                    lambda e, jc=jc, rsl=rsl: e.tensor_mul(
                        KVQ[:, R2T0 + jc * 256:R2T0 + (jc + 1) * 256],
                        KVQ[:, RLB0 + rsl * 256:RLB0 + (rsl + 1) * 256],
                        KVQ[:, RLB0 + rsl * 256:RLB0 + (rsl + 1) * 256]), cV)
                relu_free[rsl] = [(cV, cV.n)]
                fc_last_ev = (cV, cV.n)
                if jc == 6 and gg < NFG - 1 and gg > 0:
                    b_ln(2 * gg + 2)
                    b_ln(2 * gg + 3)
                if jc == 12 and gg < NFG - 1:
                    b_tr(2 * gg + 2)
                    b_tr(2 * gg + 3)
                    h2w_arr[gg + 1] = [(cS, cS.n), (cV, cV.n)]
                if jc == 24 and gg < NFG - 1:
                    emit_t2(gg + 1)
                if gg == 0:
                    if jc >= 2:
                        wproj_p2_tr(jc - 2)
                    wproj_p2_wef(jc)

            if not wproj_started[0]:
                wproj_p2_tr(30)
                wproj_p2_tr(31)
                wproj_started[0] = True
                wproj_all[0] = [(cS, cS.n), (cV, cV.n)]

            for ui in range(2):
                bp = (3, 4) if ui == 0 else (2, 5)

                def mpj(e, ui=ui, bp=bp):
                    for jc in range(JC):
                        e.matmul(pb[bp[0]][:, 0:512],
                                 KVQ[:, R2T0 + jc * 256 + ui * 128:
                                     R2T0 + jc * 256 + (ui + 1) * 128],
                                 KVQ[:, WPJT0 + jc * 1024:
                                     WPJT0 + jc * 1024 + 512],
                                 start=(jc == 0), stop=(jc == JC - 1))
                        e.matmul(pb[bp[1]][:, 0:512],
                                 KVQ[:, R2T0 + jc * 256 + ui * 128:
                                     R2T0 + jc * 256 + (ui + 1) * 128],
                                 KVQ[:, WPJT0 + jc * 1024 + 512:
                                     WPJT0 + (jc + 1) * 1024],
                                 start=(jc == 0), stop=(jc == JC - 1))

                w0 = [fc_last_ev]
                if ui == 0 and gg == 0:
                    w0 += wproj_all[0]
                t_n = tjob(list(bp), w0, mpj)
                sl, sl_ldw = tile_slots[ui]
                tg = tiles[ui]
                tt = tg % NT
                for hf in range(2):
                    tsl = hf

                    def ev1(e, bpk=bp[hf], tsl=tsl, s=s, hf=hf):
                        e.tensor_tensor(
                            KVQ[:, TMP0 + tsl * 512:TMP0 + (tsl + 1) * 512],
                            pb[bpk][:, 0:512],
                            KVQ[:, GB0 + s * 1024 + hf * 512:
                                GB0 + s * 1024 + (hf + 1) * 512],
                            ALU.mult)

                    job("v", [(cT, t_n)] + list(sl_ldw) + tmp_free[tsl],
                        ev1, cV)
                    bread(bp[hf], cV)
                    v_t1b = cV.n
                    osl = hf
                    job("v", [(cV, v_t1b)] + outst_free[osl],
                        lambda e, tsl=tsl, osl=osl, sl=sl, hf=hf: e.tensor_add(
                            outst[osl][:, :],
                            KVQ[:, TMP0 + tsl * 512:TMP0 + (tsl + 1) * 512],
                            xs[sl][:, hf * 512:(hf + 1) * 512]), cV)
                    tmp_free[tsl] = [(cV, cV.n)]
                    xs_free[sl].append((cV, cV.n))
                    dma("sc", cOS[osl],
                        yout[s, tt * 128:(tt + 1) * 128,
                             hf * 512:(hf + 1) * 512],
                        outst[osl][:, :], waits=[(cV, cV.n)])
                    outst_free[osl] = [(cOS[osl], cOS[osl].n)]
            r2_free[0] = [(cT, cT.n)]
            if gg < NFG - 2:
                b_load(2 * gg + 4)
                b_load(2 * gg + 5)

        job("sy", [(cOS[0], cOS[0].n), (cOS[1], cOS[1].n)],
            lambda e: e.nop(), None)

        with nc.Block() as block:
            @block.sync
            def _(eng):
                for ek, fn in ops:
                    if ek == "sy":
                        fn(eng)

            @block.tensor
            def _(eng):
                for ek, fn in ops:
                    if ek == "t":
                        fn(eng)

            @block.vector
            def _(eng):
                for ek, fn in ops:
                    if ek == "v":
                        fn(eng)

            @block.scalar
            def _(eng):
                for ek, fn in ops:
                    if ek in ("s", "sc"):
                        fn(eng)

            @block.gpsimd
            def _(eng):
                for ek, fn in ops:
                    if ek == "g":
                        fn(eng)

    return nc


_NC_CACHE = {}


def kernel(L0_attn_x, L0_attn_Wq, L0_attn_bq, L0_attn_Wk, L0_attn_bk,
           L0_attn_Wv, L0_attn_bv, L0_ffn_fast_b, L0_ffn_w_fc, L0_ffn_w_proj,
           L0_ffn_u, L0_ffn_v, L0_ffn_gate, L0_c, L0_wmod):
    if "nc" not in _NC_CACHE:
        _NC_CACHE["nc"] = build()
    nc = _NC_CACHE["nc"]

    f32 = np.float32
    x = np.ascontiguousarray(np.asarray(L0_attn_x, dtype=f32))
    gate = np.asarray(L0_ffn_gate, dtype=f32).reshape(1, 1)
    shared = {
        "wq": np.ascontiguousarray(np.asarray(L0_attn_Wq, f32)),
        "wk": np.ascontiguousarray(np.asarray(L0_attn_Wk, f32)),
        "wv": np.ascontiguousarray(np.asarray(L0_attn_Wv, f32)),
        "bq": np.ascontiguousarray(np.asarray(L0_attn_bq, f32)),
        "bk": np.ascontiguousarray(np.asarray(L0_attn_bk, f32)),
        "bv": np.ascontiguousarray(np.asarray(L0_attn_bv, f32)),
        "wmod": np.ascontiguousarray(np.asarray(L0_wmod, f32)),
        "wfc": np.ascontiguousarray(np.asarray(L0_ffn_w_fc, f32)),
        "wpj": np.ascontiguousarray(np.asarray(L0_ffn_w_proj, f32)),
        "uu": np.ascontiguousarray(np.asarray(L0_ffn_u, f32)),
        "vlr": np.ascontiguousarray(np.asarray(L0_ffn_v, f32)),
        "fb": np.ascontiguousarray(np.asarray(L0_ffn_fast_b, f32)),
        "gate16": np.ascontiguousarray(np.broadcast_to(gate, (16, 1))),
        "id_bf": np.eye(128, dtype=ml_dtypes.bfloat16),
        "id_f32": np.eye(16, dtype=f32),
        "mask_bf": np.triu(np.full((128, 128), -1.0e30, dtype=f32), 1).astype(
            ml_dtypes.bfloat16),
    }
    cfull = np.ascontiguousarray(np.asarray(L0_c, f32))
    in_maps = []
    for core in range(8):
        m = dict(shared)
        m["xin"] = np.ascontiguousarray(x[2 * core:2 * core + 2])
        m["cvec"] = np.ascontiguousarray(cfull[2 * core:2 * core + 2])
        in_maps.append(m)

    res = run_bass_kernel_spmd(nc, in_maps, core_ids=list(range(8)))
    y = np.concatenate([r["yout"] for r in res.results], axis=0)
    return y.reshape(16, 2048, 1024).astype(np.float32, copy=False)


# version 38
# speedup vs baseline: 1.0758x; 1.0758x over previous
import numpy as np
import ml_dtypes
import concourse.bass as bass
import concourse.mybir as mybir
from concourse.bass_utils import run_bass_kernel_spmd
from contextlib import ExitStack

F32 = mybir.dt.float32
BF16 = mybir.dt.bfloat16
AF = mybir.ActivationFunctionType
ALU = mybir.AluOpType
AX = mybir.AxisListType

D, DC, P, NT, SEQ = 1024, 8, 2048, 16, 2
H, JC = 4096, 32
NG = 4
NFG = 16

K0, V0, Q0 = 0, 16384, 32768
WPJT0, R2T0, H20, GB0 = 0, 32768, 40960, 45056
RLB0, TMP0 = 47104, 47616
WQT0, WKT0, WVT0, WST0, GA0, BV0 = 0, 8192, 16384, 24576, 28672, 30720


class Ctr:
    __slots__ = ("sem", "n")

    def __init__(self, sem):
        self.sem = sem
        self.n = 0


def build():
    nc = bass.Bass("TRN2", target_bir_lowering=False)

    xin = nc.dram_tensor("xin", [SEQ, P, D], F32, kind="ExternalInput")
    wq = nc.dram_tensor("wq", [D, D], F32, kind="ExternalInput")
    wk = nc.dram_tensor("wk", [D, D], F32, kind="ExternalInput")
    wv = nc.dram_tensor("wv", [D, D], F32, kind="ExternalInput")
    bq = nc.dram_tensor("bq", [D], F32, kind="ExternalInput")
    bk = nc.dram_tensor("bk", [D], F32, kind="ExternalInput")
    bv = nc.dram_tensor("bv", [D], F32, kind="ExternalInput")
    wmod = nc.dram_tensor("wmod", [D, 6 * D], F32, kind="ExternalInput")
    cvec = nc.dram_tensor("cvec", [SEQ, D], F32, kind="ExternalInput")
    wfc = nc.dram_tensor("wfc", [H, D], F32, kind="ExternalInput")
    wpj = nc.dram_tensor("wpj", [D, H], F32, kind="ExternalInput")
    uu = nc.dram_tensor("uu", [H, 16], F32, kind="ExternalInput")
    vlr = nc.dram_tensor("vlr", [16, D], F32, kind="ExternalInput")
    fb = nc.dram_tensor("fb", [16, 16], F32, kind="ExternalInput")
    gate16 = nc.dram_tensor("gate16", [16, 1], F32, kind="ExternalInput")
    id_bf = nc.dram_tensor("id_bf", [128, 128], BF16, kind="ExternalInput")
    id_f32 = nc.dram_tensor("id_f32", [16, 16], F32, kind="ExternalInput")
    mask_bf = nc.dram_tensor("mask_bf", [128, 128], BF16, kind="ExternalInput")
    yout = nc.dram_tensor("yout", [SEQ, P, D], F32, kind="ExternalOutput")
    x1s = nc.dram_tensor("x1s", [SEQ, P, D], BF16)

    ops = []

    with ExitStack() as ctx:
        en = ctx.enter_context

        KVQ = en(nc.sbuf_tensor("KVQ", [128, 49152], BF16))
        WPJ = en(nc.sbuf_tensor("WPJ", [128, 32768], BF16))
        C = en(nc.sbuf_tensor("C", [128, 4096], BF16))
        STG = en(nc.sbuf_tensor("STG", [128, 2048], F32))
        xs = [en(nc.sbuf_tensor(f"xs{i}", [128, 1024], BF16)) for i in range(4)]
        outst = [en(nc.sbuf_tensor(f"outst{i}", [128, 512], F32)) for i in range(2)]
        x1r = [en(nc.sbuf_tensor(f"x1r{i}", [128, 512], BF16)) for i in range(2)]
        xh = [en(nc.sbuf_tensor(f"xh{i}", [128, 1024], BF16)) for i in range(2)]
        we1 = en(nc.sbuf_tensor("we1", [128, 1024], BF16))
        Pb = en(nc.sbuf_tensor("Pb", [128, 2048], BF16))
        Pb2 = en(nc.sbuf_tensor("Pb2", [128, 1792], BF16))
        PT = en(nc.sbuf_tensor("PT", [128, 512], BF16))
        msk = en(nc.sbuf_tensor("msk", [128, 128], BF16))
        idb = en(nc.sbuf_tensor("idb", [128, 128], BF16))
        idf = en(nc.sbuf_tensor("idf", [16, 16], F32))
        onesb = en(nc.sbuf_tensor("onesb", [1, 128], BF16))
        bq32 = en(nc.sbuf_tensor("bq32", [128, 8], F32))
        bkT = en(nc.sbuf_tensor("bkT", [128, 8], F32))
        modT = en(nc.sbuf_tensor("modT", [128, 96], F32))
        caP1 = en(nc.sbuf_tensor("caP1", [128, 16], F32))
        saT = en(nc.sbuf_tensor("saT", [128, 16], F32))
        cbP1 = en(nc.sbuf_tensor("cbP1", [128, 16], F32))
        sbT = en(nc.sbuf_tensor("sbT", [128, 16], F32))
        gaTc = en(nc.sbuf_tensor("gaTc", [128, 16], F32))
        gbTc = en(nc.sbuf_tensor("gbTc", [128, 16], F32))
        scT = en(nc.sbuf_tensor("scT", [128, 16], BF16))
        scT32 = en(nc.sbuf_tensor("scT32", [128, 16], F32))
        vlrT = en(nc.sbuf_tensor("vlrT", [128, 128], BF16))
        fbT = en(nc.sbuf_tensor("fbT", [16, 16], BF16))
        fb_bf = en(nc.sbuf_tensor("fb_bf", [16, 16], BF16))
        gate_sb = en(nc.sbuf_tensor("gate_sb", [16, 1], F32))
        stt = [en(nc.sbuf_tensor(f"stt{i}", [128, 8], F32)) for i in range(2)]
        lsum = en(nc.sbuf_tensor("lsum", [128, 8], F32))
        lred = [en(nc.sbuf_tensor(f"lred{i}", [128, 1], F32)) for i in range(2)]
        linv = [en(nc.sbuf_tensor(f"linv{i}", [128, 1], F32)) for i in range(2)]
        tsc = [en(nc.sbuf_tensor(f"tsc{i}", [128, 4], F32)) for i in range(4)]
        scpj = en(nc.sbuf_tensor("scpj", [128, 8], F32))
        pabs = en(nc.sbuf_tensor("pabs", [128, 8], F32))
        kon = en(nc.sbuf_tensor("kon", [128, 2], F32))
        pb = [en(nc.psum_tensor(f"pb{i}", [128, 512], F32)) for i in range(6)]
        pbh = [en(nc.psum_tensor(f"pbh{i}", [128, 1024], BF16)) for i in range(2)]

        cW = Ctr(en(nc.semaphore("sW")))
        cT = Ctr(en(nc.semaphore("sT")))
        cV = Ctr(en(nc.semaphore("sV")))
        cS = Ctr(en(nc.semaphore("sS")))
        cX = [Ctr(en(nc.semaphore(f"sX{i}"))) for i in range(4)]
        cXB = [Ctr(en(nc.semaphore(f"sXB{i}"))) for i in range(4)]
        cSP = [Ctr(en(nc.semaphore(f"sSP{i}"))) for i in range(3)]
        cWS = [Ctr(en(nc.semaphore(f"sWS{i}"))) for i in range(4)]
        cST = [Ctr(en(nc.semaphore(f"sST{i}"))) for i in range(2)]
        cOS = [Ctr(en(nc.semaphore(f"sOS{i}"))) for i in range(2)]
        cWM = Ctr(en(nc.semaphore("sWM")))
        cBV = Ctr(en(nc.semaphore("sBV")))
        cFB = Ctr(en(nc.semaphore("sFB")))
        cVL = Ctr(en(nc.semaphore("sVL")))
        cU = Ctr(en(nc.semaphore("sU")))
        cGC = Ctr(en(nc.semaphore("sGC")))

        def _coal(waits):
            coal = {}
            for (c, v) in waits:
                assert v <= c.n, f"wait ahead of emission: {v} > {c.n}"
                if v > 0:
                    prev = coal.get(id(c))
                    coal[id(c)] = (c, v if prev is None else max(v, prev[1]))
            return tuple(coal.values())

        class _Rec:
            __slots__ = ("e", "last")

            def __init__(self, e):
                self.e = e
                self.last = None

            def __getattr__(self, name):
                m = getattr(self.e, name)

                def call(*a, **k):
                    r = m(*a, **k)
                    self.last = r
                    return r

                return call

        def job(ek, waits, body, inc):
            if ek == "v":
                waits = list(waits) + [(cV, cV.n)]
            elif ek in ("s", "sc"):
                waits = list(waits) + [(cS, cS.n)]
            elif ek == "g" and inc is cGC:
                waits = list(waits) + [(cGC, cGC.n)]
            wl = _coal(waits)
            if inc is not None:
                inc.n += 1

            def fn(eng, _w=wl, _b=body, _i=inc):
                for (c, v) in _w:
                    eng.wait_ge(c.sem, v)
                rec = _Rec(eng)
                _b(rec)
                if _i is not None:
                    rec.last.then_inc(_i.sem, 1)

            ops.append((ek, fn))

        def dma(ek, ctr, dst, src, waits=()):
            wl = _coal(waits)
            ctr.n += 16

            def fn(eng, _w=wl, _d=dst, _s=src, _c=ctr):
                for (c, v) in _w:
                    eng.wait_ge(c.sem, v)
                eng.dma_start(_d, _s).then_inc(_c.sem, 16)

            ops.append((ek, fn))

        bankw = [[] for _ in range(8)]

        def tjob(banks, waits, body):
            w = list(waits)
            for b in banks:
                w += bankw[b]
            job("t", w, body, cT)
            for b in banks:
                bankw[b] = []
            return cT.n

        def bread(b, ctr):
            bankw[b].append((ctr, ctr.n))

        xs_free = [[] for _ in range(4)]
        xs_i = [0]

        def xload(ek, src, extra=()):
            sl = xs_i[0] % 4
            xs_i[0] += 1
            fam = cX if ek == "g" else cXB
            dma(ek, fam[sl], xs[sl][:, :], src,
                waits=tuple(xs_free[sl]) + tuple(extra))
            xs_free[sl] = []
            return sl, (fam[sl], fam[sl].n)

        dma("sy", cW, idb[:, :], id_bf[:, :])
        dma("sy", cW, idf[:, :], id_f32[:, :])
        dma("sy", cW, msk[:, :], mask_bf[:, :])
        dma("sy", cW, STG[0:2, 0:1024], cvec[:, :])
        dma("sy", cW, outst[0][0:8, 0:128], bq.ap().rearrange("(a b) -> a b", a=8))
        dma("sy", cW, gate_sb[:, :], gate16[:, :])
        w_const = cW.n

        for c in range(DC):
            dma("g", cWM, KVQ[:, c * 4096:c * 4096 + 2048],
                wmod[c * 128:(c + 1) * 128, 0:2048])
            dma("g", cWM, KVQ[:, c * 4096 + 2048:(c + 1) * 4096],
                wmod[c * 128:(c + 1) * 128, 3072:5120])
        dma("g", cBV, WPJ[0:1, BV0:BV0 + 1024], bv.ap()[None, :])
        dma("g", cFB, fb_bf[:, :], fb[:, :])

        def konst(e):
            e.memset(onesb[:, :], 1.0)
            e.memset(kon[:, 0:1], 1e-5)
            e.memset(kon[:, 1:2], 1.0 / 32.0)
        job("v", [], konst, cV)
        v_ones = cV.n

        job("s", [(cW, w_const)],
            lambda e: e.activation(STG[0:2, 1024:2048], STG[0:2, 0:1024], AF.Silu),
            cS)
        s_silu = cS.n
        for c in range(DC):
            t_n = tjob([4], [(cS, s_silu)],
                       lambda e, c=c: e.matmul(
                           pb[4][0:128, 0:2],
                           STG[0:2, 1024 + c * 128:1024 + (c + 1) * 128],
                           idf[0:2, 0:2], is_transpose=True,
                           start=True, stop=True))
            def scev(e, c=c):
                e.tensor_copy(scT[:, 2 * c:2 * c + 2], pb[4][0:128, 0:2])
                e.tensor_copy(scT32[:, 2 * c:2 * c + 2], pb[4][0:128, 0:2])

            job("v", [(cT, t_n)], scev, cV)
            bread(4, cV)
        v_sct = cV.n

        t_n = tjob([5], [(cW, w_const)],
                   lambda e: e.matmul(pb[5][0:128, 0:8], outst[0][0:8, 0:128],
                                      idf[0:8, 0:8], is_transpose=True,
                                      start=True, stop=True))
        job("s", [(cT, t_n), (cV, v_ones)],
            lambda e: e.activation(bq32[:, :], pb[5][0:128, 0:8], AF.Identity,
                                   scale=kon[:, 1:2]), cS)
        bread(5, cS)
        dma("sy", cW, outst[0][0:8, 0:128],
            bk.ap().rearrange("(a b) -> a b", a=8), waits=[(cT, t_n)])
        w_bk = cW.n
        t_n = tjob([5], [(cW, w_bk)],
                   lambda e: e.matmul(pb[5][0:128, 0:8], outst[0][0:8, 0:128],
                                      idf[0:8, 0:8], is_transpose=True,
                                      start=True, stop=True))
        job("v", [(cT, t_n)],
            lambda e: e.tensor_copy(bkT[:, :], pb[5][0:128, 0:8]), cV)
        bread(5, cV)
        v_bk = cV.n
        outst_prologue_T = cT.n

        mrow_free = [[], []]
        for jgi, jg in enumerate((0, 1, 2, 3, 6, 7, 8, 9)):
            bki = jgi % 2
            mbase = jg * 512 if jg < 4 else 2048 + (jg - 6) * 512

            def mmod(e, mbase=mbase, bki=bki):
                for c in range(DC):
                    e.matmul(pb[bki][0:2, 0:512], scT[:, 2 * c:2 * c + 2],
                             KVQ[:, c * 4096 + mbase:c * 4096 + mbase + 512],
                             start=(c == 0), stop=(c == DC - 1))

            t_n = tjob([bki], [(cWM, 256), (cV, v_sct)], mmod)
            mr = jgi % 2
            job("v", [(cT, t_n)] + mrow_free[mr],
                lambda e, mr=mr, bki=bki: e.tensor_copy(
                    STG[0:2, mr * 512:(mr + 1) * 512], pb[bki][0:2, 0:512]), cV)
            bread(bki, cV)
            v_row = cV.n
            for ui in range(4):
                j0 = jg * 512 + ui * 128
                which, cc0 = j0 // 1024, (j0 % 1024) // 128
                col = (which * 8 + cc0) * 2
                t_n = tjob([2], [(cV, v_row)],
                           lambda e, mr=mr, ui=ui: e.matmul(
                               pb[2][0:128, 0:2],
                               STG[0:2, mr * 512 + ui * 128:
                                   mr * 512 + (ui + 1) * 128],
                               idf[0:2, 0:2], is_transpose=True,
                               start=True, stop=True))
                job("v", [(cT, t_n)],
                    lambda e, col=col: e.tensor_copy(modT[:, col:col + 2],
                                                     pb[2][0:128, 0:2]), cV)
                bread(2, cV)
            mrow_free[mr] = [(cV, cV.n)]
        v_mod = cV.n

        early_x = [xload("g", xin[0, 0:128, :]), xload("g", xin[0, 128:256, :])]

        wst_free = [[], []]
        half_i = [0]
        for wsrc, wdst in ((wq, WQT0), (wk, WKT0), (wv, WVT0)):
            for hf in range(2):
                ws = half_i[0] % 4
                w2 = half_i[0] % 2
                half_i[0] += 1
                stga, base = (WPJ, WST0) if w2 == 0 else (KVQ, Q0)
                dma("g", cWS[ws],
                    stga[:, base:base + 4096],
                    bass.AP(wsrc, hf * 512 * 1024,
                            [[1024, 128], [131072, 4], [1, 1024]]),
                    waits=wst_free[w2])
                wst_free[w2] = []
                g_ld0 = (cWS[ws], cWS[ws].n)
                for f4 in range(4):
                    f = hf * 4 + f4
                    g_ld = g_ld0
                    for hb in range(2):

                        def trb(e, sl=f4, hb=hb, stga=stga, base=base):
                            for k4 in range(4):
                                c = hb * 4 + k4
                                e.matmul(
                                    pbh[hb][0:128, k4 * 128:(k4 + 1) * 128],
                                    stga[:, base + sl * 1024 + c * 128:
                                         base + sl * 1024 + (c + 1) * 128],
                                    idb[:, :], is_transpose=True,
                                    start=(k4 == 0), stop=(k4 == 3))

                        t_n = tjob([6 + hb], [g_ld], trb)
                        ek = "v" if hb == 0 else "s"
                        cc = cV if ek == "v" else cS

                        if ek == "s":
                            def ev(e, wdst=wdst, f=f, hb=hb):
                                for k4 in range(4):
                                    c = hb * 4 + k4
                                    e.copy(
                                        WPJ[:, wdst + c * 1024 + f * 128:
                                            wdst + c * 1024 + (f + 1) * 128],
                                        pbh[hb][0:128,
                                                k4 * 128:(k4 + 1) * 128])
                        else:
                            def ev(e, wdst=wdst, f=f, hb=hb):
                                for k4 in range(4):
                                    c = hb * 4 + k4
                                    e.tensor_copy(
                                        WPJ[:, wdst + c * 1024 + f * 128:
                                            wdst + c * 1024 + (f + 1) * 128],
                                        pbh[hb][0:128,
                                                k4 * 128:(k4 + 1) * 128])

                        job(ek, [(cT, t_n)], ev, cc)
                        bread(6 + hb, cc)
                    wst_free[w2].append((cT, cT.n))
        t_guard = cT.n
        stgR = [[(cT, t_guard)], [(cT, t_guard)]]
        row_guard = [[]]
        for which in (2, 5):
            mm_w = []
            for c in range(DC):
                sl = c % 2
                dma("sy" if sl == 0 else "sc", cST[sl],
                    STG[:, sl * 1024:(sl + 1) * 1024],
                    wmod[c * 128:(c + 1) * 128, which * 1024:(which + 1) * 1024],
                    waits=stgR[sl])
                w_ld = (cST[sl], cST[sl].n)

                def mref(e, c=c, sl=sl):
                    e.matmul(pb[0][0:2, 0:512], scT32[:, 2 * c:2 * c + 2],
                             STG[:, sl * 1024:sl * 1024 + 512],
                             start=(c == 0), stop=(c == DC - 1))
                    e.matmul(pb[1][0:2, 0:512], scT32[:, 2 * c:2 * c + 2],
                             STG[:, sl * 1024 + 512:(sl + 1) * 1024],
                             start=(c == 0), stop=(c == DC - 1))

                bw = [w_ld, (cV, v_sct)]
                if c == 0:
                    bw += bankw[0] + bankw[1]
                    bankw[0] = []
                    bankw[1] = []
                job("t", bw, mref, cT)
                stgR[sl] = [(cT, cT.n)]
            t_mm = cT.n
            for half in range(2):
                job("v", [(cT, t_mm)] + row_guard[0], 
                    lambda e, half=half: e.tensor_copy(
                        outst[1][0:2, 0:512], pb[half][0:2, 0:512]), cV)
                bread(half, cV)
                v_row = cV.n
                for ui in range(4):
                    cc0 = half * 4 + ui
                    col = (which * 8 + cc0) * 2
                    t_n = tjob([2], [(cV, v_row)],
                               lambda e, ui=ui: e.matmul(
                                   pb[2][0:128, 0:2],
                                   outst[1][0:2, ui * 128:(ui + 1) * 128],
                                   idf[0:2, 0:2], is_transpose=True,
                                   start=True, stop=True))
                    job("v", [(cT, t_n)],
                        lambda e, col=col: e.tensor_copy(
                            modT[:, col:col + 2], pb[2][0:128, 0:2]), cV)
                    bread(2, cV)
                row_guard[0] = [(cT, cT.n)]
        v_modref = cV.n

        def derive(e):
            e.tensor_scalar(caP1[:, :], modT[:, 16:32], 1.0, None, ALU.add)
            e.tensor_copy(saT[:, :], modT[:, 0:16])
            e.tensor_scalar(cbP1[:, :], modT[:, 64:80], 1.0, None, ALU.add)
            e.tensor_copy(sbT[:, :], modT[:, 48:64])

        job("v", [(cV, v_mod)], derive, cV)

        def clampg(base, dst):
            def f(e):
                for h8 in range(2):
                    src0 = modT[:, base + h8 * 8:base + h8 * 8 + 8]
                    d8 = dst[:, h8 * 8:h8 * 8 + 8]
                    e.tensor_scalar(d8, src0, 0.25, None, ALU.max)
                    e.tensor_scalar(scpj[:, :], src0, -0.25, None, ALU.min)
                    e.tensor_single_scalar(pabs[:, :], src0, 0.0, ALU.is_ge)
                    e.drain()
                    e.tensor_sub(d8, d8, scpj[:, :])
                    e.drain()
                    e.tensor_mul(d8, d8, pabs[:, :])
                    e.drain()
                    e.tensor_add(d8, d8, scpj[:, :])
                    e.drain()
            return f

        job("v", [(cV, v_mod), (cV, v_modref)], clampg(32, gaTc), cV)
        job("v", [], clampg(80, gbTc), cV)
        v_gates = cV.n

        def bcast_gate(src_t, arena, dst_base, first_waits):
            job("v", list(first_waits) + [(cV, v_gates), (cV, v_ones)],
                lambda e, src_t=src_t: e.tensor_copy(scT[:, :], src_t[:, :]),
                cV)
            v_cp = cV.n
            v1s = {}
            for col in range(16):
                pp = col % 2
                t1 = tjob([6 + pp], [(cV, v_cp)],
                          lambda e, col=col, pp=pp: e.matmul(
                              pbh[pp][0:1, 0:128], scT[:, col:col + 1],
                              idb[:, :], is_transpose=True,
                              start=True, stop=True))
                job("v", [(cT, t1)] + list(first_waits),
                    lambda e, pp=pp, col=col: e.tensor_copy(
                        Pb[0:1, col * 128:(col + 1) * 128],
                        pbh[pp][0:1, 0:128]), cV)
                bread(6 + pp, cV)
                v1s[col] = cV.n
            for q in range(4):
                b = q // 2
                cbase = (q % 2) * 4
                bk = 3 + (q % 2)
                cols = [(cbase + j) * 2 + b for j in range(4)]

                def mbc(e, cols=cols, bk=bk):
                    for j, col in enumerate(cols):
                        e.matmul(pb[bk][:, j * 128:(j + 1) * 128],
                                 onesb[:, :],
                                 Pb[0:1, col * 128:(col + 1) * 128],
                                 start=(j == 0), stop=(j == 3))

                t2 = tjob([bk], [(cV, v1s[c]) for c in cols], mbc)
                dst = arena[:, dst_base + b * 1024 + cbase * 128:
                            dst_base + b * 1024 + (cbase + 4) * 128]
                if q % 2 == 0:
                    job("v", [(cT, t2)],
                        lambda e, dst=dst, bk=bk:
                        e.tensor_copy(dst, pb[bk][:, 0:512]), cV)
                    bread(bk, cV)
                else:
                    job("s", [(cT, t2)],
                        lambda e, dst=dst, bk=bk:
                        e.copy(dst, pb[bk][:, 0:512]), cS)
                    bread(bk, cS)
            return [(cV, cV.n), (cS, cS.n)]

        gab_done = bcast_gate(gaTc, WPJ, GA0, [])

        v_derive = v_gates

        xh_free = [[], []]
        stt_free = [[], []]

        def ln_chain(sl, ldw, r):
            if isinstance(ldw, tuple):
                ldw = [ldw]
            st = stt[r]
            job("v", list(ldw) + stt_free[r],
                lambda e, st=st, sl=sl: e.reduce_sum(st[:, 0:1], xs[sl][:, :],
                                                     AX.X), cV)
            job("s", list(ldw) + xh_free[r] + stt_free[r],
                lambda e, st=st, sl=sl, r=r: e.activation(
                    xh[r][:, :], xs[sl][:, :], AF.Square, accum_out=st[:, 1:2]),
                cS)
            s_sq = cS.n

            def statf(e, st=st):
                e.tensor_scalar(st[:, 2:3], st[:, 0:1], 1.0 / 1024.0, None,
                                ALU.mult)
                e.drain()
                e.tensor_mul(st[:, 3:4], st[:, 2:3], st[:, 2:3])
                e.drain()
                e.tensor_scalar(st[:, 4:5], st[:, 1:2], 1.0 / 1024.0, st[:, 3:4],
                                ALU.mult, ALU.subtract)

            job("v", [(cS, s_sq)], statf, cV)
            job("s", [(cV, cV.n)],
                lambda e, st=st: e.activation(st[:, 5:6], st[:, 4:5], AF.Sqrt,
                                              bias=kon[:, 0:1]), cS)
            s_rt = cS.n

            def vxh(e, st=st, sl=sl, r=r):
                e.reciprocal(st[:, 6:7], st[:, 5:6])
                e.drain()
                e.tensor_scalar(xh[r][:, :], xs[sl][:, :], st[:, 2:3],
                                st[:, 6:7], ALU.subtract, ALU.mult)

            job("v", [(cS, s_rt)], vxh, cV)
            v_xh = cV.n
            xs_free[sl].append((cV, v_xh))
            stt_free[r] = [(cV, v_xh)]
            return (cV, v_xh)

        def ln_tr(vxh_w, r, scale_t, bias_t, smcol, dst_fn):
            hb = r % 2

            def trb(e, hb=hb, r=r):
                for c in range(8):
                    e.matmul(pbh[hb][0:128, c * 128:(c + 1) * 128],
                             xh[r][:, c * 128:(c + 1) * 128],
                             idb[:, :], is_transpose=True,
                             start=(c == 0), stop=(c == 7))

            t_n = tjob([6 + hb], [vxh_w], trb)
            ek = "s" if r % 2 == 0 else "v"
            cc = cS if ek == "s" else cV

            if ek == "s":
                def ev(e, hb=hb, smcol=smcol, scale_t=scale_t,
                       bias_t=bias_t):
                    for c in range(8):
                        col = c * 2 + smcol
                        e.activation(dst_fn(c),
                                     pbh[hb][0:128, c * 128:(c + 1) * 128],
                                     AF.Identity,
                                     scale=scale_t[:, col:col + 1],
                                     bias=bias_t[:, col:col + 1])
            else:
                def ev(e, hb=hb, smcol=smcol, scale_t=scale_t,
                       bias_t=bias_t):
                    for c in range(8):
                        col = c * 2 + smcol
                        e.tensor_scalar(dst_fn(c),
                                        pbh[hb][0:128,
                                                c * 128:(c + 1) * 128],
                                        scale_t[:, col:col + 1],
                                        bias_t[:, col:col + 1],
                                        ALU.mult, ALU.add)

            job(ek, [(cT, t_n), (cV, v_derive)], ev, cc)
            bread(6 + hb, cc)
            xh_free[r] = [(cT, t_n)]

        stg_free = [[(cT, cT.n)], [(cT, cT.n)]]
        we_free = [[]]
        we1_war = [[]]
        wef_free = [[], [], []]
        wef_ready = {}
        ts_free = [[], [], [], []]
        pabs_free = [[]]
        wefp_free = [[], []]

        tern_i = [0]

        def tern_transposes(wef_ap, dst_fn, first_waits):
            hb = tern_i[0] % 2
            tern_i[0] += 1

            def trb(e, hb=hb, wef_ap=wef_ap):
                for c in range(8):
                    e.matmul(pbh[hb][0:128, c * 128:(c + 1) * 128],
                             wef_ap[:, c * 128:(c + 1) * 128],
                             idb[:, :], is_transpose=True,
                             start=(c == 0), stop=(c == 7))

            t_n = tjob([6 + hb], list(first_waits), trb)
            ek = "s" if hb == 0 else "v"
            cc = cS if ek == "s" else cV

            if ek == "s":
                def ev(e, hb=hb, dst_fn=dst_fn):
                    for c in range(8):
                        e.copy(dst_fn(c),
                               pbh[hb][0:128, c * 128:(c + 1) * 128])
            else:
                def ev(e, hb=hb, dst_fn=dst_fn):
                    for c in range(8):
                        e.tensor_copy(dst_fn(c),
                                      pbh[hb][0:128,
                                              c * 128:(c + 1) * 128])

            job(ek, [(cT, t_n)], ev, cc)
            bread(6 + hb, cc)
            return t_n

        WEFW = [0, 1024, 3072]

        def wfc_wef(jc, post_attn):
            sl = jc % 2
            stg = STG[:, sl * 1024:(sl + 1) * 1024]
            dma("sy", cST[sl], stg,
                wfc[jc * 128:(jc + 1) * 128, :],
                waits=stg_free[sl])
            w_ld = (cST[sl], cST[sl].n)
            ts_ = tsc[jc % 4]
            dumpA = C[:, 2048:3072]
            wp = jc % 3
            wefb = C[:, WEFW[wp]:WEFW[wp] + 1024]
            dw = [(cT, attn_T_done[1])] if post_attn else []
            job("s", [w_ld] + dw + ts_free[jc % 4],
                lambda e, stg=stg, dumpA=dumpA, ts_=ts_: e.activation(
                    dumpA, stg, AF.Abs, accum_out=ts_[:, 0:1]), cS)
            s_ab = cS.n

            def scf(e, ts_=ts_):
                e.tensor_scalar(ts_[:, 1:2], ts_[:, 0:1], 1.0 / 1024.0, 1e-5,
                                ALU.mult, ALU.max)
                e.drain()
                e.tensor_scalar(ts_[:, 2:3], ts_[:, 1:2], 0.5, None, ALU.mult)
                e.tensor_scalar(ts_[:, 0:1], ts_[:, 1:2], -1.0, None, ALU.mult)
                e.drain()
                e.tensor_scalar(ts_[:, 3:4], ts_[:, 2:3], -1.0, None, ALU.mult)

            job("v", [(cS, s_ab)], scf, cV)
            job("v", wef_free[wp],
                lambda e, stg=stg, ts_=ts_, wefb=wefb: e.tensor_scalar(
                    wefb, stg, ts_[:, 2:3], ts_[:, 1:2], ALU.is_ge,
                    ALU.mult), cV)
            job("v", [],
                lambda e, stg=stg, ts_=ts_: e.tensor_scalar(
                    we1[:, :], stg, ts_[:, 3:4], ts_[:, 0:1], ALU.is_le,
                    ALU.mult), cV)
            job("v", [],
                lambda e, wefb=wefb: e.tensor_add(wefb, wefb, we1[:, :]), cV)
            v_wef = cV.n
            stg_free[sl] = [(cV, v_wef)]
            ts_free[jc % 4] = [(cV, v_wef)]
            we_free[0] = [(cV, v_wef)]
            wef_ready[jc] = (cV, v_wef)

        def wfc_tr(jc, post_attn):
            wp = jc % 3
            wefb = C[:, WEFW[wp]:WEFW[wp] + 1024]
            dw = [(cT, attn_T_done[1])] if post_attn else []
            last_tr = tern_transposes(
                wefb,
                (lambda c, jc=jc: WPJ[:, jc * 1024 + c * 128:
                                      jc * 1024 + (c + 1) * 128]),
                [wef_ready[jc]] + dw)
            wef_free[wp] = [(cT, last_tr)]

        def wfc_chunk(jc, post_attn):
            wfc_wef(jc, post_attn)
            wfc_tr(jc, post_attn)

        def wproj_p1_chunk(q_i):
            ec, qq = q_i // 4, q_i % 4
            sl = q_i % 2
            stg = STG[:, sl * 1024:(sl + 1) * 1024]
            dma("sy" if sl == 0 else "sc", cST[sl], stg,
                wpj[ec * 128:(ec + 1) * 128, qq * 1024:(qq + 1) * 1024],
                waits=stg_free[sl])
            w_ld = (cST[sl], cST[sl].n)
            job("s", [w_ld, (cV, v_gates)] + list(kqv_ready[0]) +
                (pabs_free[0] if qq == 0 else []),
                lambda e, stg=stg, qq=qq: e.activation(
                    C[:, 2048:3072], stg, AF.Abs,
                    accum_out=pabs[:, qq:qq + 1]), cS)
            stg_free[sl] = [(cS, cS.n)]
            if qq == 3:
                def comb(e, ec=ec):
                    e.reduce_sum(pabs[:, 4:5], pabs[:, 0:4], AX.X)
                    e.drain()
                    e.tensor_scalar(scpj[:, ec:ec + 1], pabs[:, 4:5],
                                    1.0 / 4096.0, 1e-5, ALU.mult, ALU.max)

                job("v", [(cS, cS.n)], comb, cV)
                pabs_free[0] = [(cV, cV.n)]

        p2_ready = {}

        def wproj_p2_wef(q_i):
            ec, qq = q_i // 4, q_i % 4
            sl = q_i % 2
            stg = STG[:, sl * 1024:(sl + 1) * 1024]
            dma("sc", cST[sl], stg,
                wpj[ec * 128:(ec + 1) * 128, qq * 1024:(qq + 1) * 1024],
                waits=stg_free[sl])
            w_ld = (cST[sl], cST[sl].n)
            ts_ = tsc[q_i % 2]
            ts_free[q_i % 2] = [(cV, cV.n)]

            def scf(e, ts_=ts_, ec=ec):
                e.tensor_scalar(ts_[:, 2:3], scpj[:, ec:ec + 1], 0.5, None,
                                ALU.mult)
                e.tensor_scalar(ts_[:, 0:1], scpj[:, ec:ec + 1], -1.0, None,
                                ALU.mult)
                e.tensor_copy(ts_[:, 1:2], scpj[:, ec:ec + 1])
                e.drain()
                e.tensor_scalar(ts_[:, 3:4], ts_[:, 2:3], -1.0, None, ALU.mult)

            job("v", [w_ld], scf, cV)
            wp = q_i % 2
            wefb = Pb[:, wp * 1024:(wp + 1) * 1024]
            extra_pb = ([(cT, attn_T_done[1])] + gbb_done) if q_i < 2 else []
            job("v", wefp_free[wp] + extra_pb,
                lambda e, stg=stg, ts_=ts_, wefb=wefb: e.tensor_scalar(
                    wefb, stg, ts_[:, 2:3], ts_[:, 1:2], ALU.is_ge,
                    ALU.mult), cV)
            job("v", we1_war[0],
                lambda e, stg=stg, ts_=ts_: e.tensor_scalar(
                    we1[:, :], stg, ts_[:, 3:4], ts_[:, 0:1], ALU.is_le,
                    ALU.mult), cV)
            job("v", [],
                lambda e, wefb=wefb: e.tensor_add(wefb, wefb, we1[:, :]), cV)
            v_wef = cV.n
            stg_free[sl] = [(cV, v_wef)]
            p2_ready[q_i] = (cV, v_wef)

        def wproj_p2_tr(q_i):
            ec, qq = q_i // 4, q_i % 4
            wp = q_i % 2
            wefb = Pb[:, wp * 1024:(wp + 1) * 1024]
            last_tr = tern_transposes(
                wefb,
                (lambda js, qq=qq, ec=ec: KVQ[:, WPJT0 + (qq * 8 + js) * 1024 +
                                              ec * 128:
                                              WPJT0 + (qq * 8 + js) * 1024 +
                                              (ec + 1) * 128]),
                [p2_ready[q_i]])
            wefp_free[wp] = [(cT, last_tr)]

        x1r_free = [[] for _ in range(2)]
        kqv_ready = [None, None]
        attn_T_done = [0, 0]

        for s in range(SEQ):
            ev_k = ev_q = ev_v = (cV, 0)

            ch_state = {}

            def a_chain(t, s=s, ch_state=ch_state):
                if s == 0 and t < 2:
                    sl, ldw = early_x[t]
                else:
                    sl, ldw = xload("g", xin[s, t * 128:(t + 1) * 128, :])
                ch_state[t] = ln_chain(sl, ldw, t % 2)

            def a_tr(t, s=s, ch_state=ch_state):
                ln_tr(ch_state[t], t % 2, caP1, saT, s,
                      (lambda c, ti=t % 4: C[:, c * 512 + ti * 128:
                                             c * 512 + ti * 128 + 128]))

            a_chain(0)
            a_chain(1)
            for g in range(NG):
                a_tr(4 * g)
                a_tr(4 * g + 1)
                a_chain(4 * g + 2)
                a_chain(4 * g + 3)
                a_tr(4 * g + 2)
                a_tr(4 * g + 3)
                h_ready = [(cS, cS.n), (cV, cV.n)]
                if g < NG - 1:
                    a_chain(4 * g + 4)
                    a_chain(4 * g + 5)
                for f in range(DC):
                    bki = f % 2

                    def mmk(e, f=f, g=g, bki=bki):
                        for c in range(DC):
                            e.matmul(pb[bki][:, 0:512],
                                     WPJ[:, WKT0 + c * 1024 + f * 128:
                                         WKT0 + c * 1024 + (f + 1) * 128],
                                     C[:, c * 512:(c + 1) * 512],
                                     start=(c == 0), stop=(c == DC - 1))

                    t_n = tjob([bki], h_ready if f == 0 else [], mmk)
                    job("s", [(cT, t_n), (cV, v_bk)],
                        lambda e, f=f, g=g, bki=bki: e.activation(
                            KVQ[:, K0 + f * 2048 + g * 512:
                                K0 + f * 2048 + g * 512 + 512],
                            pb[bki][:, 0:512], AF.Identity, bias=bkT[:, f:f + 1]),
                        cS)
                    bread(bki, cS)
                ev_k = (cS, cS.n)
                for f in range(DC):
                    bki = 2 + (f % 2)

                    def mmq(e, f=f, g=g, bki=bki):
                        for c in range(DC):
                            e.matmul(pb[bki][:, 0:512],
                                     WPJ[:, WQT0 + c * 1024 + f * 128:
                                         WQT0 + c * 1024 + (f + 1) * 128],
                                     C[:, c * 512:(c + 1) * 512],
                                     start=(c == 0), stop=(c == DC - 1))

                    t_n = tjob([bki], [], mmq)
                    job("s", [(cT, t_n)],
                        lambda e, f=f, g=g, bki=bki: e.activation(
                            KVQ[:, Q0 + f * 2048 + g * 512:
                                Q0 + f * 2048 + g * 512 + 512],
                            pb[bki][:, 0:512], AF.Identity, scale=kon[:, 1:2],
                            bias=bq32[:, f:f + 1]), cS)
                    bread(bki, cS)
                ev_q = (cS, cS.n)
                for ui in range(4):
                    tabs = g * 4 + ui
                    for hf in range(2):
                        bki = 4 + ((ui * 2 + hf) % 2)

                        def mmv(e, ui=ui, hf=hf, bki=bki):
                            e.matmul(pb[bki][:, 0:512], onesb[:, :],
                                     WPJ[0:1, BV0 + hf * 512:
                                         BV0 + (hf + 1) * 512],
                                     start=True, stop=False)
                            for c in range(DC):
                                e.matmul(pb[bki][:, 0:512],
                                         C[:, c * 512 + ui * 128:
                                           c * 512 + (ui + 1) * 128],
                                         WPJ[:, WVT0 + c * 1024 + hf * 512:
                                             WVT0 + c * 1024 + (hf + 1) * 512],
                                         start=False, stop=(c == DC - 1))

                        w0 = ([(cBV, 16), (cV, v_ones)] + gab_done) if (
                            s == 0 and g == 0 and ui == 0 and hf == 0) else []
                        t_n = tjob([bki], w0, mmv)
                        job("v", [(cT, t_n)],
                            lambda e, tabs=tabs, hf=hf, s=s, bki=bki:
                            e.tensor_tensor(
                                KVQ[:, V0 + tabs * 1024 + hf * 512:
                                    V0 + tabs * 1024 + (hf + 1) * 512],
                                pb[bki][:, 0:512],
                                WPJ[:, GA0 + s * 1024 + hf * 512:
                                    GA0 + s * 1024 + (hf + 1) * 512],
                                ALU.mult), cV)
                        bread(bki, cV)
                ev_v = (cV, cV.n)
            kqv_ready[s] = [ev_k, ev_q, ev_v]

            pt_sl_free = [[], []]
            pt_read_prev = {0: None, 1: None}
            pend = None
            wfc_jc = 0
            wfc_tr_jc = 0
            wproj_q = 0

            def emit_ptpv(st8, s=s):
                pi, p_exp_n, p_ldw, p_sl, p_ldone, p_buf = st8
                t_pv = 0
                njj = pi + 1
                ngr = (njj + 3) // 4
                for g4 in range(ngr):
                    j0 = g4 * 4
                    j1 = min(j0 + 4, njj)
                    hb = g4 % 2
                    PTs = PT

                    def trg(e, j0=j0, j1=j1, hb=hb, p_buf=p_buf):
                        for j in range(j0, j1):
                            e.matmul(pbh[hb][0:128, (j - j0) * 128:
                                             (j - j0 + 1) * 128],
                                     p_buf[:, j * 128:(j + 1) * 128],
                                     idb[:, :], is_transpose=True,
                                     start=(j == j0), stop=(j == j1 - 1))

                    t_n = tjob([6 + hb], [(cS, p_exp_n)] if g4 == 0 else [], trg)
                    if s == 0 and g4 % 2 == 0:
                        job("v", [(cT, t_n)] + pt_sl_free[0],
                            lambda e, j0=j0, j1=j1, hb=hb, PTs=PTs:
                            e.tensor_copy(
                                PTs[:, 0:(j1 - j0) * 128],
                                pbh[hb][0:128, 0:(j1 - j0) * 128]), cV)
                        bread(6 + hb, cV)
                        v_pt = (cV, cV.n)
                    else:
                        job("s", [(cT, t_n)] + pt_sl_free[0],
                            lambda e, j0=j0, j1=j1, hb=hb, PTs=PTs: e.copy(
                                PTs[:, 0:(j1 - j0) * 128],
                                pbh[hb][0:128, 0:(j1 - j0) * 128]), cS)
                        bread(6 + hb, cS)
                        v_pt = (cS, cS.n)

                    def mpv(e, j0=j0, j1=j1, pi=pi, PTs=PTs):
                        for j in range(j0, j1):
                            e.matmul(pb[4][:, 0:512],
                                     PTs[:, (j - j0) * 128:(j - j0 + 1) * 128],
                                     KVQ[:, V0 + j * 1024:V0 + j * 1024 + 512],
                                     start=(j == 0), stop=(j == pi))
                            e.matmul(pb[5][:, 0:512],
                                     PTs[:, (j - j0) * 128:(j - j0 + 1) * 128],
                                     KVQ[:, V0 + j * 1024 + 512:
                                         V0 + (j + 1) * 1024],
                                     start=(j == 0), stop=(j == pi))

                    t_pv = tjob([4, 5] if g4 == 0 else [], [v_pt], mpv)
                    pt_sl_free[0] = [(cT, t_pv)]
                tr_done = (cT, cT.n)
                for hf in range(2):
                    rsl = (2 * pi + hf) % 2
                    job("v", [(cT, t_pv), p_ldone, p_ldw] + x1r_free[rsl],
                        lambda e, hf=hf, rsl=rsl, pi=pi, p_sl=p_sl:
                        e.scalar_tensor_tensor(
                            x1r[rsl][:, :], pb[4 + hf][:, 0:512],
                            linv[pi % 2][:, :],
                            xs[p_sl][:, hf * 512:(hf + 1) * 512],
                            ALU.mult, ALU.add), cV)
                    bread(4 + hf, cV)
                    dma("g", cSP[rsl],
                        x1s[s, pi * 128:(pi + 1) * 128,
                            hf * 512:(hf + 1) * 512],
                        x1r[rsl][:, :], waits=[(cV, cV.n)])
                    x1r_free[rsl] = [(cSP[rsl], cSP[rsl].n)]
                xs_free[p_sl].append((cV, cV.n))
                return tr_done

            for i in range(NT):
                L = (i + 1) * 128
                nb = (L + 511) // 512
                sb0 = (i % 2) * 2 if nb <= 2 else 0
                sl, ldw = xload("g", xin[s, i * 128:(i + 1) * 128, :])

                db_, off_ = (i * 128) // 512, (i * 128) % 512
                sc_w = (list(kqv_ready[s]) + [(cW, w_const)]) if i == 0 \
                    else []
                t_scs = []
                for kb in range(nb):
                    n_kb = min(512, L - kb * 512)
                    dg = (kb == db_)

                    def mms(e, i=i, kb=kb, n_kb=n_kb, dg=dg, sb0=sb0,
                            off_=off_):
                        for f in range(DC):
                            e.matmul(pb[sb0 + kb][:, 0:n_kb],
                                     KVQ[:, Q0 + f * 2048 + i * 128:
                                         Q0 + f * 2048 + (i + 1) * 128],
                                     KVQ[:, K0 + f * 2048 + kb * 512:
                                         K0 + f * 2048 + kb * 512 + n_kb],
                                     start=(f == 0),
                                     stop=(not dg and f == DC - 1))
                        if dg:
                            e.matmul(pb[sb0 + kb][:, off_:off_ + 128],
                                     idb[:, :], msk[:, :], start=False,
                                     stop=True)

                    t_scs.append(tjob([sb0 + kb], sc_w if kb == 0 else [],
                                      mms))
                Pbi = Pb2 if (i % 2 == 1 and i != 15) else Pb
                key = 1 if Pbi is Pb2 else 0
                if pend is not None:
                    pt_read_prev[1 if pend[5] is Pb2 else 0] = emit_ptpv(pend)
                    pend = None
                pw = pt_read_prev[key]
                exp_w = [pw] if pw else []
                for kb in range(nb):
                    n_kb = min(512, L - kb * 512)
                    job("s", [(cT, t_scs[kb])] + (exp_w if kb == 0 else []),
                        lambda e, kb=kb, n_kb=n_kb, sb0=sb0, Pbi=Pbi,
                        lo=(i % 2) * 4:
                        e.activation(
                            Pbi[:, kb * 512:kb * 512 + n_kb],
                            pb[sb0 + kb][:, 0:n_kb], AF.Exp,
                            accum_out=lsum[:, lo + kb:lo + kb + 1]), cS)
                    bread(sb0 + kb, cS)
                exp_n = cS.n

                def vl(e, i=i, nb=nb):
                    lo = (i % 2) * 4
                    e.reduce_sum(lred[i % 2][:, :], lsum[:, lo:lo + nb], AX.X)
                    e.drain()
                    e.reciprocal(linv[i % 2][:, :], lred[i % 2][:, :])

                job("v", [(cS, exp_n)], vl, cV)
                pend = (i, exp_n, ldw, sl, (cV, cV.n), Pbi)

                if s == 1 and i >= 1:
                    for _ in range(2):
                        if wfc_tr_jc < wfc_jc - 1:
                            wfc_tr(wfc_tr_jc, False)
                            wfc_tr_jc += 1
                    for _ in range(2):
                        if wfc_jc < 28:
                            wfc_wef(wfc_jc, False)
                            wfc_jc += 1
                if s == 0 and wproj_q < 32:
                    for _ in range(2):
                        if wproj_q < 32:
                            wproj_p1_chunk(wproj_q)
                            wproj_q += 1
            emit_ptpv(pend)
            attn_T_done[s] = cT.n
            if s == 1:
                while wfc_jc < 28:
                    wfc_wef(wfc_jc, False)
                    wfc_jc += 1
                while wfc_tr_jc < 28:
                    wfc_tr(wfc_tr_jc, False)
                    wfc_tr_jc += 1

        for jc in (28, 29, 30):
            wfc_wef(jc, True)
        wfc_tr(28, True)
        wfc_wef(31, True)
        for jc in (29, 30, 31):
            wfc_tr(jc, True)
        wfc_all = [(cS, cS.n), (cV, cV.n)]
        we_free[0] = [(cV, cV.n)]
        sp_final = [(c, c.n) for c in cSP]

        t_n = tjob([6], [(cFB, 16), (cT, attn_T_done[1])],
                   lambda e: e.matmul(pbh[0][0:16, 0:16], fb_bf[:, :],
                                      idb[0:16, 0:16], is_transpose=True,
                                      start=True, stop=True))
        job("v", [(cT, t_n)],
            lambda e: e.tensor_copy(fbT[:, :], pbh[0][0:16, 0:16]), cV)
        bread(6, cV)
        v_fbt = cV.n
        dma("g", cVL, Pb2[0:16, 0:1024], vlr[:, :],
            waits=[(cT, attn_T_done[1])])
        dma("g", cU, we1[:, 0:512],
            bass.AP(uu, 0, [[16, 128], [2048, 32], [1, 16]]), waits=we_free[0])

        def mfbv(e):
            e.matmul(pb[0][0:16, 0:512], fbT[:, :], Pb2[0:16, 0:512],
                     start=True, stop=True)
            e.matmul(pb[1][0:16, 0:512], fbT[:, :], Pb2[0:16, 512:1024],
                     start=True, stop=True)

        t_n = tjob([0, 1], [(cV, v_fbt), (cVL, 16)], mfbv)

        def efbv(e):
            e.copy(C[0:16, 3072:3584], pb[0][0:16, 0:512])
            e.copy(C[0:16, 3584:4096], pb[1][0:16, 0:512])

        job("s", [(cT, t_n)], efbv, cS)
        bread(0, cS)
        bread(1, cS)
        s_fbv = cS.n
        for gidx in range(2):
            hb = gidx % 2

            def trv(e, gidx=gidx, hb=hb):
                for k in range(4):
                    c = gidx * 4 + k
                    e.matmul(pbh[hb][0:128, k * 16:(k + 1) * 16],
                             C[0:16, 3072 + c * 128:3072 + (c + 1) * 128],
                             idb[0:16, 0:16], is_transpose=True,
                             start=(k == 0), stop=(k == 3))

            t_n = tjob([6 + hb], [(cS, s_fbv)], trv)
            job("v", [(cT, t_n)],
                lambda e, gidx=gidx, hb=hb: e.tensor_copy(
                    vlrT[:, gidx * 64:(gidx + 1) * 64],
                    pbh[hb][0:128, 0:64]), cV)
            bread(6 + hb, cV)
        for g8 in range(8):
            hb = g8 % 2

            def tru(e, g8=g8, hb=hb):
                for k in range(4):
                    jc = g8 * 4 + k
                    e.matmul(pbh[hb][0:16, k * 128:(k + 1) * 128],
                             we1[:, jc * 16:(jc + 1) * 16],
                             idb[:, :], is_transpose=True,
                             start=(k == 0), stop=(k == 3))

            t_n = tjob([6 + hb], [(cU, 16)], tru)
            job("v", [(cT, t_n)],
                lambda e, g8=g8, hb=hb: e.tensor_copy(
                    C[0:16, g8 * 512:(g8 + 1) * 512],
                    pbh[hb][0:16, 0:512]), cV)
            bread(6 + hb, cV)
        we1_war[0] = [(cT, cT.n)]
        v_lrsetup = cV.n

        gbb_done = bcast_gate(gbTc, KVQ, GB0, [(cT, attn_T_done[1])])

        r2_free = [[]]
        relu_free = [[], []]
        tmp_free = [[], []]
        outst_free = [[(cT, outst_prologue_T)], []]
        wproj_started = [False]
        wproj_all = [None]
        t12_free = [[]]

        b_state = {}
        b_slots = {}

        def b_load(tg):
            s2 = tg // NT
            tt = tg % NT
            sl = xs_i[0] % 4
            xs_i[0] += 1
            w = tuple(xs_free[sl]) + tuple(sp_final)
            dma("g", cX[sl], xs[sl][:, 0:512],
                x1s[s2, tt * 128:(tt + 1) * 128, 0:512], waits=w)
            dma("sy", cXB[sl], xs[sl][:, 512:1024],
                x1s[s2, tt * 128:(tt + 1) * 128, 512:1024], waits=w)
            xs_free[sl] = []
            b_slots[tg] = (sl, [(cX[sl], cX[sl].n), (cXB[sl], cXB[sl].n)])

        def b_ln(tg):
            sl, ldw = b_slots[tg]
            b_state[tg] = ln_chain(sl, ldw, tg % 2)

        def b_chain(tg):
            b_load(tg)
            b_ln(tg)

        def b_tr(tg):
            s2 = tg // NT
            b2 = (tg // 2) % 2
            k2 = tg % 2
            ln_tr(b_state[tg], tg % 2, cbP1, sbT, s2,
                  (lambda c, k2=k2, b2=b2:
                   KVQ[:, H20 + b2 * 2048 + c * 256 + k2 * 128:
                       H20 + b2 * 2048 + c * 256 + k2 * 128 + 128]))

        b_chain(0)
        b_chain(1)
        b_tr(0)
        b_tr(1)
        h2w_arr = {0: [(cS, cS.n), (cV, cV.n)]}
        b_chain(2)
        b_chain(3)
        t2g_done_arr = {}

        def emit_t2(gg):
            b2 = gg % 2
            T2G = 48640 + (gg % 2) * 256

            def mt2(e, b2=b2):
                for c in range(DC):
                    e.matmul(pb[2][0:16, 0:256], vlrT[:, c * 16:(c + 1) * 16],
                             KVQ[:, H20 + b2 * 2048 + c * 256:
                                 H20 + b2 * 2048 + (c + 1) * 256],
                             start=(c == 0), stop=(c == DC - 1))

            w0 = list(h2w_arr[gg]) + t12_free[0]
            if gg == 0:
                w0 += [(cV, v_lrsetup)]
            t_n = tjob([2], w0, mt2)
            job("s", [(cT, t_n), (cW, w_const)],
                lambda e, T2G=T2G: e.activation(
                    KVQ[0:16, T2G:T2G + 256], pb[2][0:16, 0:256],
                    AF.Identity, scale=gate_sb[:, :]), cS)
            bread(2, cS)
            t2g_done_arr[gg] = (cS, cS.n)
            t12_free[0] = [(cT, cT.n)]

        emit_t2(0)
        for gg in range(NFG):
            s = gg // 8
            b2 = gg % 2
            tiles = [gg * 2, gg * 2 + 1]
            tile_slots = [b_slots[2 * gg], b_slots[2 * gg + 1]]
            T2G = 48640 + (gg % 2) * 256
            t2g_done = t2g_done_arr[gg]

            fc_last_ev = None
            FCB = (0, 1, 2, 5)
            for jc in range(JC):
                bki = FCB[jc % 4]

                def mfc(e, jc=jc, b2=b2, bki=bki, T2G=T2G):
                    e.matmul(pb[bki][:, 0:256], C[0:16, jc * 128:(jc + 1) * 128],
                             KVQ[0:16, T2G:T2G + 256], start=True, stop=False)
                    for c in range(DC):
                        e.matmul(pb[bki][:, 0:256],
                                 WPJ[:, jc * 1024 + c * 128:
                                     jc * 1024 + (c + 1) * 128],
                                 KVQ[:, H20 + b2 * 2048 + c * 256:
                                     H20 + b2 * 2048 + (c + 1) * 256],
                                 start=False, stop=(c == DC - 1))

                w0 = [t2g_done] if jc == 0 else []
                if gg == 0 and jc == 0:
                    w0 += wfc_all + gbb_done
                t_n = tjob([bki], w0, mfc)
                rsl = jc % 2
                job("s", [(cT, t_n)] + relu_free[rsl],
                    lambda e, rsl=rsl, bki=bki: e.activation(
                        KVQ[:, RLB0 + rsl * 256:RLB0 + (rsl + 1) * 256],
                        pb[bki][:, 0:256], AF.Relu), cS)
                bread(bki, cS)
                s_rl = cS.n
                job("v", ([(cS, s_rl)] + r2_free[0]) if jc == 0 else
                    [(cS, s_rl)],
                    lambda e, jc=jc, rsl=rsl: e.tensor_mul(
                        KVQ[:, R2T0 + jc * 256:R2T0 + (jc + 1) * 256],
                        KVQ[:, RLB0 + rsl * 256:RLB0 + (rsl + 1) * 256],
                        KVQ[:, RLB0 + rsl * 256:RLB0 + (rsl + 1) * 256]), cV)
                relu_free[rsl] = [(cV, cV.n)]
                fc_last_ev = (cV, cV.n)
                if jc == 6 and gg < NFG - 1 and gg > 0:
                    b_ln(2 * gg + 2)
                    b_ln(2 * gg + 3)
                if jc == 12 and gg < NFG - 1:
                    b_tr(2 * gg + 2)
                    b_tr(2 * gg + 3)
                    h2w_arr[gg + 1] = [(cS, cS.n), (cV, cV.n)]
                if jc == 24 and gg < NFG - 1:
                    emit_t2(gg + 1)
                if gg == 0:
                    if jc >= 2:
                        wproj_p2_tr(jc - 2)
                    wproj_p2_wef(jc)

            if not wproj_started[0]:
                wproj_p2_tr(30)
                wproj_p2_tr(31)
                wproj_started[0] = True
                wproj_all[0] = [(cS, cS.n), (cV, cV.n)]

            for ui in range(2):
                bp = (3, 4) if ui == 0 else (2, 5)

                def mpj(e, ui=ui, bp=bp):
                    for jc in range(JC):
                        e.matmul(pb[bp[0]][:, 0:512],
                                 KVQ[:, R2T0 + jc * 256 + ui * 128:
                                     R2T0 + jc * 256 + (ui + 1) * 128],
                                 KVQ[:, WPJT0 + jc * 1024:
                                     WPJT0 + jc * 1024 + 512],
                                 start=(jc == 0), stop=(jc == JC - 1))
                        e.matmul(pb[bp[1]][:, 0:512],
                                 KVQ[:, R2T0 + jc * 256 + ui * 128:
                                     R2T0 + jc * 256 + (ui + 1) * 128],
                                 KVQ[:, WPJT0 + jc * 1024 + 512:
                                     WPJT0 + (jc + 1) * 1024],
                                 start=(jc == 0), stop=(jc == JC - 1))

                w0 = [fc_last_ev]
                if ui == 0 and gg == 0:
                    w0 += wproj_all[0]
                t_n = tjob(list(bp), w0, mpj)
                sl, sl_ldw = tile_slots[ui]
                tg = tiles[ui]
                tt = tg % NT
                for hf in range(2):
                    tsl = hf

                    def ev1(e, bpk=bp[hf], tsl=tsl, s=s, hf=hf):
                        e.tensor_tensor(
                            KVQ[:, TMP0 + tsl * 512:TMP0 + (tsl + 1) * 512],
                            pb[bpk][:, 0:512],
                            KVQ[:, GB0 + s * 1024 + hf * 512:
                                GB0 + s * 1024 + (hf + 1) * 512],
                            ALU.mult)

                    job("v", [(cT, t_n)] + list(sl_ldw) + tmp_free[tsl],
                        ev1, cV)
                    bread(bp[hf], cV)
                    v_t1b = cV.n
                    osl = hf
                    job("v", [(cV, v_t1b)] + outst_free[osl],
                        lambda e, tsl=tsl, osl=osl, sl=sl, hf=hf: e.tensor_add(
                            outst[osl][:, :],
                            KVQ[:, TMP0 + tsl * 512:TMP0 + (tsl + 1) * 512],
                            xs[sl][:, hf * 512:(hf + 1) * 512]), cV)
                    tmp_free[tsl] = [(cV, cV.n)]
                    xs_free[sl].append((cV, cV.n))
                    dma("sc", cOS[osl],
                        yout[s, tt * 128:(tt + 1) * 128,
                             hf * 512:(hf + 1) * 512],
                        outst[osl][:, :], waits=[(cV, cV.n)])
                    outst_free[osl] = [(cOS[osl], cOS[osl].n)]
            r2_free[0] = [(cT, cT.n)]
            if gg < NFG - 2:
                b_load(2 * gg + 4)
                b_load(2 * gg + 5)

        job("sy", [(cOS[0], cOS[0].n), (cOS[1], cOS[1].n)],
            lambda e: e.nop(), None)

        with nc.Block() as block:
            @block.sync
            def _(eng):
                for ek, fn in ops:
                    if ek == "sy":
                        fn(eng)

            @block.tensor
            def _(eng):
                for ek, fn in ops:
                    if ek == "t":
                        fn(eng)

            @block.vector
            def _(eng):
                for ek, fn in ops:
                    if ek == "v":
                        fn(eng)

            @block.scalar
            def _(eng):
                for ek, fn in ops:
                    if ek in ("s", "sc"):
                        fn(eng)

            @block.gpsimd
            def _(eng):
                for ek, fn in ops:
                    if ek == "g":
                        fn(eng)

    return nc


_NC_CACHE = {}


def kernel(L0_attn_x, L0_attn_Wq, L0_attn_bq, L0_attn_Wk, L0_attn_bk,
           L0_attn_Wv, L0_attn_bv, L0_ffn_fast_b, L0_ffn_w_fc, L0_ffn_w_proj,
           L0_ffn_u, L0_ffn_v, L0_ffn_gate, L0_c, L0_wmod):
    if "nc" not in _NC_CACHE:
        _NC_CACHE["nc"] = build()
    nc = _NC_CACHE["nc"]

    f32 = np.float32
    x = np.ascontiguousarray(np.asarray(L0_attn_x, dtype=f32))
    gate = np.asarray(L0_ffn_gate, dtype=f32).reshape(1, 1)
    shared = {
        "wq": np.ascontiguousarray(np.asarray(L0_attn_Wq, f32)),
        "wk": np.ascontiguousarray(np.asarray(L0_attn_Wk, f32)),
        "wv": np.ascontiguousarray(np.asarray(L0_attn_Wv, f32)),
        "bq": np.ascontiguousarray(np.asarray(L0_attn_bq, f32)),
        "bk": np.ascontiguousarray(np.asarray(L0_attn_bk, f32)),
        "bv": np.ascontiguousarray(np.asarray(L0_attn_bv, f32)),
        "wmod": np.ascontiguousarray(np.asarray(L0_wmod, f32)),
        "wfc": np.ascontiguousarray(np.asarray(L0_ffn_w_fc, f32)),
        "wpj": np.ascontiguousarray(np.asarray(L0_ffn_w_proj, f32)),
        "uu": np.ascontiguousarray(np.asarray(L0_ffn_u, f32)),
        "vlr": np.ascontiguousarray(np.asarray(L0_ffn_v, f32)),
        "fb": np.ascontiguousarray(np.asarray(L0_ffn_fast_b, f32)),
        "gate16": np.ascontiguousarray(np.broadcast_to(gate, (16, 1))),
        "id_bf": np.eye(128, dtype=ml_dtypes.bfloat16),
        "id_f32": np.eye(16, dtype=f32),
        "mask_bf": np.triu(np.full((128, 128), -1.0e30, dtype=f32), 1).astype(
            ml_dtypes.bfloat16),
    }
    cfull = np.ascontiguousarray(np.asarray(L0_c, f32))
    in_maps = []
    for core in range(8):
        m = dict(shared)
        m["xin"] = np.ascontiguousarray(x[2 * core:2 * core + 2])
        m["cvec"] = np.ascontiguousarray(cfull[2 * core:2 * core + 2])
        in_maps.append(m)

    res = run_bass_kernel_spmd(nc, in_maps, core_ids=list(range(8)))
    y = np.concatenate([r["yout"] for r in res.results], axis=0)
    return y.reshape(16, 2048, 1024).astype(np.float32, copy=False)
